# version 13
# baseline (speedup 1.0000x reference)
import contextlib
import numpy as np
import ml_dtypes

import concourse.bass as bass
import concourse.mybir as mybir
from concourse.bass_utils import run_bass_kernel_spmd

F32 = mybir.dt.float32
BF16 = mybir.dt.bfloat16
AF = mybir.ActivationFunctionType
ALU = mybir.AluOpType

D = 2048
S = 4096
B = 4
NH = 16
G = 4
NKT = 32
DCH = 16
SCALE = float(1.0 / np.sqrt(128.0))
SHIFT = -12.0
OWN_BLOCKS = {0: [(0, 7), (1, 6)], 1: [(2, 5), (3, 4)]}


class Ctl:
    DMA_SEMS = ("sq", "gq", "xta", "xtb", "gu0", "gu1", "dn0", "dn1",
                "qo0", "qo1", "sta", "stb", "stc", "std")

    def __init__(self, nc, es):
        self.sems = {}
        self.counts = {}
        for name in ("pe", "act", "dve", "pool") + self.DMA_SEMS:
            self.sems[name] = es.enter_context(nc.semaphore("s_" + name))
            self.counts[name] = 0

    def inc(self, instr, name, n=1):
        self.counts[name] += n
        instr.then_inc(self.sems[name], n)
        return self.counts[name]

    def dinc(self, instr, name):
        return self.inc(instr, name, 16)

    def wait(self, eng, name, val):
        if val and val > 0:
            eng.wait_ge(self.sems[name], val)

    def snapshot(self):
        return dict(self.counts)

    def restore(self, snap):
        self.counts = dict(snap)


def build_program(stop_after=None, dumps=()):
    nc = bass.Bass("TRN2", target_bir_lowering=False)
    es = contextlib.ExitStack()

    x_ext = nc.dram_tensor("x_ext", [S, D], F32, kind="ExternalInput")
    x_own = nc.dram_tensor("x_own", [2048, D], F32, kind="ExternalInput")
    wq_r = nc.dram_tensor("wq_r", [NH, 128, DCH, 128], F32, kind="ExternalInput")
    wk_n = nc.dram_tensor("wk", [D, 512], F32, kind="ExternalInput")
    wv_n = nc.dram_tensor("wv", [D, 512], F32, kind="ExternalInput")
    wo_r = nc.dram_tensor("wo_r", [G, 8, 128, 2, 4, 128], F32, kind="ExternalInput")
    bo_r = nc.dram_tensor("bo_r", [128, 16], F32, kind="ExternalInput")
    wgu_r = nc.dram_tensor("wgu_r", [128, 128, DCH, 128], F32, kind="ExternalInput")
    wdn_r = nc.dram_tensor("wdn_r", [8, 16, 128, 8, 128], F32, kind="ExternalInput")
    idents = nc.dram_tensor("idents", [128, 128], F32, kind="ExternalInput")
    identb = nc.dram_tensor("identb", [128, 128], BF16, kind="ExternalInput")
    maskbig = nc.dram_tensor("maskbig", [128, 896], BF16, kind="ExternalInput")
    out = nc.dram_tensor("out", [2048, D], F32, kind="ExternalOutput")

    dbg = {}
    for name, cols, dt in dumps:
        dbg[name] = nc.dram_tensor("dbg_" + name, [128, cols], dt, kind="ExternalOutput")

    ctl = Ctl(nc, es)
    pe, act, dve, sy, gq = nc.tensor, nc.scalar, nc.vector, nc.sync, nc.gpsimd

    sbuf = lambda n, sh, dt: es.enter_context(nc.sbuf_tensor(n, sh, dt))
    psum = lambda n, sh, dt: es.enter_context(nc.psum_tensor(n, sh, dt))

    kT = [sbuf(f"kT{g}", [128, NKT * 128], BF16) for g in range(G)]
    vT = [sbuf(f"vT{g}", [128, NKT * 128], BF16) for g in range(G)]
    ident_f = sbuf("ident_f", [128, 128], F32)
    maskb = sbuf("maskb", [128, 896], BF16)
    bo_s = sbuf("bo_s", [128, 16], F32)
    ones_b = sbuf("ones_b", [128, 128], BF16)
    stat = sbuf("stat", [128, 32], F32)
    cpool = sbuf("cpool", [128, 4], F32)
    xnl = sbuf("xnl", [128, 1024], F32)
    xt = [sbuf(f"xt{i}", [128, 1024], F32) for i in range(2)]

    pA = psum("pA", [128, 512], F32)
    pB = psum("pB", [128, 512], F32)
    pC = psum("pC", [128, 512], F32)
    pD = psum("pD", [128, 512], F32)
    pS = [psum(f"pS{i}", [128, 512], F32) for i in range(2)]
    pDen = psum("pDen", [128, 512], F32)


    def dump(name, src_ap):
        if name in dbg:
            t = ctl.dinc(sy.dma_start(out=dbg[name].ap()[:, :], in_=src_ap), "sq")
            sy.wait_ge(ctl.sems["sq"], t)

    ctl.dinc(sy.dma_start(out=ident_f[:, :], in_=idents.ap()[:, :]), "sq")
    ctl.dinc(sy.dma_start(out=maskb[:, :], in_=maskbig.ap()[:, :]), "sq")
    ctl.dinc(sy.dma_start(out=bo_s[:, :], in_=bo_r.ap()[:, :]), "sq")
    dve.memset(ones_b[:, :], 1.0)
    dve.memset(cpool[:, 0:1], 1e-5)
    dve.memset(cpool[:, 1:2], SHIFT)
    sy.wait_ge(ctl.sems["sq"], ctl.counts["sq"])
    nc.all_engine_barrier()

    sw_engines = [pe, act, dve, gq]
    sw_idx = [e.compute_val(e.partition_id() % 2) for e in sw_engines]

    pt_evac = [0, 0]

    def emit_ln_tile(x_dram, r0, dest3d, prev, bufs=None):
        bxt, bstat, bxnl, lds = bufs if bufs is not None else (xt, stat, xnl, ("xta", "xtb"))
        ctl.wait(sy, "act", prev.get("xapply", 0))
        t0 = ctl.dinc(sy.dma_start(out=bxt[0][:, :],
                                   in_=x_dram.ap()[r0:r0 + 128, 0:1024]), lds[0])
        t1 = ctl.dinc(sy.dma_start(out=bxt[1][:, :],
                                   in_=x_dram.ap()[r0:r0 + 128, 1024:2048]), lds[1])
        ctl.wait(dve, "act", prev.get("xapply", 0))
        ctl.wait(dve, lds[0], t0)
        dve.bn_stats(bstat[:, 0:6], bxt[0][:, 0:512])
        dve.bn_stats(bstat[:, 6:12], bxt[0][:, 512:1024])
        ctl.wait(dve, lds[1], t1)
        dve.bn_stats(bstat[:, 12:18], bxt[1][:, 0:512])
        tb = ctl.inc(dve.bn_stats(bstat[:, 18:24], bxt[1][:, 512:1024]), "dve")
        ctl.wait(dve, "dve", tb)
        ta = ctl.inc(dve.bn_aggr(bstat[:, 24:26], bstat[:, 0:24]), "dve")
        ctl.wait(act, "dve", ta)
        ts = ctl.inc(act.activation(bstat[:, 26:27], bstat[:, 25:26], AF.Sqrt,
                                    bias=cpool[:, 0:1], scale=1.0), "act")
        ctl.wait(dve, "act", ts)
        tr = ctl.inc(dve.reciprocal(bstat[:, 27:28], bstat[:, 26:27]), "dve")
        ctl.wait(dve, "dve", tr)
        tn = ctl.inc(dve.scalar_tensor_tensor(bstat[:, 28:29], bstat[:, 24:25], -1.0,
                                              bstat[:, 27:28], ALU.mult, ALU.mult), "dve")
        ctl.wait(act, "dve", tn)
        tp = prev.get("transpose", 0)
        tx = 0
        for half in range(2):
            ctl.wait(act, "pe", tp)
            tx = ctl.inc(act.activation(bxnl[:, :], bxt[half][:, :], AF.Identity,
                                        bias=bstat[:, 28:29], scale=bstat[:, 27:28]), "act")
            for q4 in range(2):
                qg = half * 2 + q4
                pq = qg % 2
                pt_bank = pS[pq]
                ctl.wait(pe, "act", tx)
                ctl.wait(pe, "act", pt_evac[pq])
                for j in range(4):
                    dcl = q4 * 4 + j
                    tp = ctl.inc(pe.transpose(pt_bank[:, j * 128:(j + 1) * 128],
                                              bxnl[:, dcl * 128:(dcl + 1) * 128],
                                              ident_f[:, :]), "pe")
                ctl.wait(act, "pe", tp)
                pt_evac[pq] = ctl.inc(act.copy(
                    dest3d(qg),
                    pt_bank[:, :].rearrange("p (a b) -> p a b", a=4)), "act")
        return {"xapply": tx, "transpose": tp, "evac": max(pt_evac)}

    xn = sbuf("xn", [128, DCH * 1024], BF16)

    with contextlib.ExitStack() as p1es:
        wkb = p1es.enter_context(nc.sbuf_tensor("wkb", [128, DCH * 512], BF16))
        wvb = p1es.enter_context(nc.sbuf_tensor("wvb", [128, DCH * 512], BF16))
        xnT = p1es.enter_context(nc.sbuf_tensor("xnT", [128, DCH * 512], BF16))
        xt2 = [p1es.enter_context(nc.sbuf_tensor(f"xt2_{i}", [128, 1024], F32))
               for i in range(2)]
        stat2 = p1es.enter_context(nc.sbuf_tensor("stat2", [128, 32], F32))
        xnl2 = p1es.enter_context(nc.sbuf_tensor("xnl2", [128, 1024], F32))
        xt3 = [p1es.enter_context(nc.sbuf_tensor(f"xt3_{i}", [128, 1024], F32))
               for i in range(2)]
        stat3 = p1es.enter_context(nc.sbuf_tensor("stat3", [128, 32], F32))
        xnl3 = p1es.enter_context(nc.sbuf_tensor("xnl3", [128, 1024], F32))
        scr = p1es.enter_context(nc.sbuf_tensor("scr", [128, 192], BF16))
        scr_init = ctl.inc(gq.memset(scr[:, :], 0.0), "pool")
        bufsets = [(xt, stat, xnl, ("xta", "xtb")),
                   (xt2, stat2, xnl2, ("qo0", "qo1")),
                   (xt3, stat3, xnl3, ("stc", "std"))]
        xnT3 = xnT[:, :].rearrange("p (d c) -> p d c", d=DCH)
        tw = 0
        for d in range(DCH):
            ctl.dinc(gq.dma_start(out=wkb[:, d * 512:(d + 1) * 512],
                                  in_=wk_n.ap()[d * 128:(d + 1) * 128, :]), "gq")
            tw = ctl.dinc(gq.dma_start(out=wvb[:, d * 512:(d + 1) * 512],
                                       in_=wv_n.ap()[d * 128:(d + 1) * 128, :]), "gq")

        prevs = [{}, {}, {}]
        k_free = {0: 0, 1: 0}
        v_free = {0: 0, 1: 0}
        ctl.wait(pe, "gq", tw)
        for st in range(8):
            for tt in range(4):
                r0 = st * 512 + tt * 128
                bs = (st * 4 + tt) % 3

                def dest3d(q4, tt=tt):
                    base = xnT[:, :].rearrange("p (d c) -> p d c", d=DCH)
                    return base[:, q4 * 4:(q4 + 1) * 4, tt * 128:(tt + 1) * 128]

                prevs[bs] = emit_ln_tile(x_ext, r0, dest3d, prevs[bs],
                                         bufsets[bs])
            prev = {"evac": max(pv.get("evac", 0) for pv in prevs)}
            own_map = {0: (0, 0), 2: (1, 0), 5: (1, 1), 7: (0, 1)}
            if st in own_map:
                own_pair, half = own_map[st]
                snap = ctl.snapshot()
                ends = []
                for i in nc.Switch(engines=sw_engines, index=sw_idx, n=2):
                    ctl.restore(snap)
                    if i == own_pair:
                        ctl.wait(gq, "act", prev["evac"])
                        for d_ in range(DCH):
                            ctl.inc(gq.tensor_copy(
                                xn[:, d_ * 1024 + half * 512: d_ * 1024 + (half + 1) * 512],
                                xnT[:, d_ * 512:(d_ + 1) * 512]), "pool")
                    else:
                        ctl.wait(gq, "pool", scr_init)
                        base_slot = {0: 0, 2: 16, 5: 32, 7: 48}[st]
                        for d_ in range(DCH):
                            sl = (base_slot + d_) * 2
                            ctl.inc(gq.tensor_copy(scr[:, sl:sl + 2],
                                                   scr[:, 128:130]), "pool")
                    ends.append(ctl.snapshot())
                assert ends[0] == ends[1]
                ctl.restore(ends[0])
                ctl.wait(act, "pool", ctl.counts["pool"])
            ctl.wait(pe, "act", prev["evac"])
            for g in range(G):
                bank = g % 2
                pk = pA if bank == 0 else pB
                ctl.wait(pe, "act", k_free[bank])
                tm = 0
                for d in range(DCH):
                    tm = ctl.inc(pe.matmul(pk[:, :],
                                           wkb[:, d * 512 + g * 128: d * 512 + (g + 1) * 128],
                                           xnT[:, d * 512:(d + 1) * 512],
                                           start=(d == 0), stop=(d == DCH - 1)), "pe")
                ctl.wait(act, "pe", tm)
                k_free[bank] = ctl.inc(act.copy(kT[g][:, st * 512:(st + 1) * 512], pk[:, :]), "act")
            for tt in range(4):
                bank = tt % 2
                pv = pC if bank == 0 else pD
                ctl.wait(pe, "act", v_free[bank])
                tm = 0
                for d in range(DCH):
                    tm = ctl.inc(pe.matmul(pv[:, :],
                                           xnT[:, d * 512 + tt * 128: d * 512 + (tt + 1) * 128],
                                           wvb[:, d * 512:(d + 1) * 512],
                                           start=(d == 0), stop=(d == DCH - 1)), "pe")
                ctl.wait(act, "pe", tm)
                tv = 0
                kt_idx = st * 4 + tt
                for g in range(G):
                    tv = ctl.inc(act.copy(vT[g][:, kt_idx * 128:(kt_idx + 1) * 128],
                                          pv[:, g * 128:(g + 1) * 128]), "act")
                v_free[bank] = tv

        p1_prev0 = prevs[0]

    dump("kT0", kT[0][:, :])
    dump("vT0", vT[0][:, :])
    pe_after_p1 = ctl.counts["pe"]
    act_after_p1 = ctl.counts["act"]
    if stop_after == "p1":
        nc.all_engine_barrier()
        sy.wait_ge(ctl.sems["sq"], ctl.counts["sq"])
        nc.finalize()
        return nc, es

    acc = [sbuf(f"acc{d}", [128, 1024], F32) for d in range(DCH)]
    hg = [sbuf(f"hg{c}", [128, 1024], BF16) for c in range(8)]
    ctxg = [sbuf(f"ctx{h}", [128, 1024], BF16) for h in range(4)]
    qh0 = None
    ring = [sbuf(f"wr{i}", [128, 1024], BF16) for i in range(4)]
    RSEMS = ["gu0", "gu1", "dn0", "dn1"]
    ring_state = {"next": 0, "free": [pe_after_p1] * 4}

    def ring_load(src_ap):
        i = ring_state["next"]
        ring_state["next"] = (i + 1) % len(ring)
        ctl.wait(gq, "pe", ring_state["free"][i])
        tl = ctl.dinc(gq.dma_start(out=ring[i][:, :], in_=src_ap), RSEMS[i])
        return i, tl

    def ring_done(i):
        ring_state["free"][i] = ctl.counts["pe"]

    attb = [hg[0][:, 0:512], hg[0][:, 512:1024], hg[2][:, 0:512]]
    qh = [hg[1][:, :], hg[3][:, :]]
    bcf = xt[1][:, 512:1024]
    rd = xt[1][0:1, 0:512]

    def xn_s(d, s):
        return xn[:, d * 1024 + s * 512: d * 1024 + (s + 1) * 512]

    for p in range(2):
        prev = p1_prev0 if p == 0 else {}
        if p > 0:
            ctl.wait(sy, "dve", ctl.counts["dve"])
            pt_evac[0] = ctl.counts["act"]
            pt_evac[1] = ctl.counts["act"]
            for tile in range(8):
                r0 = p * 1024 + tile * 128

                def dest3d(q4, tile=tile):
                    base = xn[:, :].rearrange("p (d c) -> p d c", d=DCH)
                    return base[:, q4 * 4:(q4 + 1) * 4, tile * 128:(tile + 1) * 128]

                prev = emit_ln_tile(x_own, r0, dest3d, prev)
        ln_evac_last = prev.get("evac", 0)
        if p == 0:
            ctl.wait(pe, "pool", ctl.counts["pool"])
        if stop_after == "ln0":
            nc.all_engine_barrier()
            dump("xn", xn[:, 0:4096])
            sy.wait_ge(ctl.sems["sq"], ctl.counts["sq"])
            nc.finalize()
            return nc, es

        h_war = 0
        psd_free = [None, None, None, None]
        psd_banks = [pC, pD, pS[0], pS[1]]
        h_tok = {}
        ctl.wait(pe, "act", max(ln_evac_last, act_after_p1))
        if p > 0:
            for s_ in ("sta", "stb"):
                ctl.wait(act, s_, ctl.counts[s_])
                ctl.wait(dve, s_, ctl.counts[s_])
            ctl.wait(pe, "dve", ctl.counts["dve"])
        for grp in range(8):
            prev_h = {}
            for c8 in range(8):
                c = grp * 8 + c8
                for which, blk in (("g", c), ("u", 64 + c)):
                    loads = [ring_load(wgu_r.ap()[blk, :, hf * 8:(hf + 1) * 8, :])
                             for hf in range(2)]
                    tmm_s = {0: 0, 1: 0}
                    for half in range(2):
                        slot, tl = loads[half]
                        ctl.wait(pe, RSEMS[slot], tl)
                        for s in range(2):
                            pg = pA if s == 0 else pB
                            if half == 0:
                                if which == "g":
                                    ctl.wait(pe, "dve", prev_h.get(s, 0))
                                else:
                                    ctl.wait(pe, "act", h_tok.get(("silu", c8, s), 0))
                            for j in range(8):
                                d = half * 8 + j
                                tmm_s[s] = ctl.inc(pe.matmul(
                                    pg[:, :],
                                    ring[slot][:, j * 128:(j + 1) * 128],
                                    xn_s(d, s),
                                    start=(d == 0), stop=(d == DCH - 1)), "pe")
                        ring_done(slot)
                    for s in range(2):
                        pg = pA if s == 0 else pB
                        if which == "g":
                            ctl.wait(act, "pe", tmm_s[s])
                            h_tok[("silu", c8, s)] = ctl.inc(act.activation(
                                hg[c8][:, s * 512:(s + 1) * 512], pg[:, :],
                                AF.Silu, bias=0.0, scale=1.0), "act")
                        else:
                            ctl.wait(dve, "pe", tmm_s[s])
                            ctl.wait(dve, "act", h_tok[("silu", c8, s)])
                            ctl.wait(dve, "pe", h_war)
                            h_tok[(c8, s)] = ctl.inc(dve.tensor_mul(
                                hg[c8][:, s * 512:(s + 1) * 512],
                                hg[c8][:, s * 512:(s + 1) * 512],
                                pg[:, :]), "dve")
                prev_h = {0: h_tok[(c8, 0)], 1: h_tok[(c8, 1)]}
            for o in range(16):
                rslot, tl = ring_load(wdn_r.ap()[grp, o, :, :, :])
                bank = o % 2
                ctl.wait(pe, RSEMS[rslot], tl)
                for s in range(2):
                    pd = psd_banks[bank * 2 + s]
                    pf = psd_free[bank * 2 + s]
                    if pf is not None:
                        ctl.wait(pe, pf[0], pf[1])
                    tmm = 0
                    for c8 in range(8):
                        ctl.wait(pe, "dve", h_tok[(c8, s)])
                        tmm = ctl.inc(pe.matmul(
                            pd[:, :],
                            ring[rslot][:, c8 * 128:(c8 + 1) * 128],
                            hg[c8][:, s * 512:(s + 1) * 512],
                            start=(c8 == 0), stop=(c8 == 7)), "pe")
                    if grp == 0:
                        ctl.wait(act, "pe", tmm)
                        te = ctl.inc(act.activation(
                            acc[o][:, s * 512:(s + 1) * 512], pd[:, :],
                            AF.Identity, bias=bo_s[:, o:o + 1], scale=1.0), "act")
                        psd_free[bank * 2 + s] = ("act", te)
                    else:
                        ctl.wait(dve, "pe", tmm)
                        te = ctl.inc(dve.tensor_add(
                            acc[o][:, s * 512:(s + 1) * 512],
                            acc[o][:, s * 512:(s + 1) * 512], pd[:, :]), "dve")
                        psd_free[bank * 2 + s] = ("dve", te)
                ring_done(rslot)
            h_war = ctl.counts["pe"]
        if stop_after == "ffn0":
            nc.all_engine_barrier()
            for i, nme in ((0, "acc0"), (8, "acc8")):
                dump(nme, acc[i][:, :])
            sy.wait_ge(ctl.sems["sq"], ctl.counts["sq"])
            nc.finalize()
            return nc, es

        p_exts = {0: [4 * (OWN_BLOCKS[0][p][0] + 1), 4 * (OWN_BLOCKS[0][p][1] + 1)],
                  1: [4 * (OWN_BLOCKS[1][p][0] + 1), 4 * (OWN_BLOCKS[1][p][1] + 1)]}

        dve_at_body_start = ctl.counts["dve"]
        act_at_body_start = ctl.counts["act"]
        ring_snap = dict(ring_state, free=list(ring_state["free"]))

        def body(pair):
            e_a, e_b = p_exts[pair]
            for g in range(G):
                ctl.wait(pe, "dve", dve_at_body_start if g == 0 else ctl.counts["dve"])
                ctl.wait(pe, "act", act_at_body_start if g == 0 else ctl.counts["act"])
                prev_q = [0, 0]
                norm = {"bank": [0, 0], "tcp": 0}
                qbc = [0]
                prev_exp_last = 0
                pending = []

                def flush_upto(qmax):
                    while pending and pending[0][0] <= qmax:
                        pending.pop(0)[1]()

                def flush_one():
                    if pending:
                        pending.pop(0)[1]()

                def flush_pending():
                    while pending:
                        pending.pop(0)[1]()

                def emit_qproj(h, qdst):
                    nonlocal prev_q
                    tmm_q = 0
                    loads = [ring_load(wq_r.ap()[h, :, hf * 8:(hf + 1) * 8, :])
                             for hf in range(2)]
                    for hf in range(2):
                        slot, tl = loads[hf]
                        ctl.wait(pe, RSEMS[slot], tl)
                        if hf == 0:
                            ctl.wait(pe, "act", prev_q[1])
                        for s in range(2):
                            pq = pA if s == 0 else pB
                            for j in range(8):
                                d = hf * 8 + j
                                tmm_q = ctl.inc(pe.matmul(
                                    pq[:, :],
                                    ring[slot][:, j * 128:(j + 1) * 128],
                                    xn_s(d, s),
                                    start=(d == 0), stop=(d == DCH - 1)), "pe")
                        ring_done(slot)
                    ctl.wait(act, "pe", tmm_q)
                    tq0 = ctl.inc(act.copy(qdst[:, 0:512], pA[:, :]), "act")
                    tq1 = ctl.inc(act.copy(qdst[:, 512:1024], pB[:, :]), "act")
                    prev_q = [tq0, tq1]
                    return tq0, tq1

                tq_next = emit_qproj(g * 4, qh[0])
                for hh in range(4):
                    h = g * 4 + hh
                    qdst = qh[hh % 2]
                    tq0, tq1 = tq_next
                    for qbi, ext in ((0, e_a), (1, e_b)):
                        qc = qbi * 512
                        exp_t = {}
                        mask_t = {}
                        ctx_t = {}
                        den_t = {}
                        pCD = [pC, pD][qbc[0] % 2]
                        qbc[0] += 1

                        def emit_ctxden(k, ext=ext, g=g, pCD=pCD):
                            if k == 0:
                                ctl.wait(pe, "dve", norm["bank"][pCD is pD])
                            if k in mask_t:
                                ctl.wait(pe, "dve", mask_t[k])
                            else:
                                ctl.wait(pe, "act", exp_t[k])
                            ctx_t[k] = ctl.inc(pe.matmul(
                                pCD[:, :],
                                vT[g][:, k * 128:(k + 1) * 128],
                                attb[k % 3],
                                start=(k == 0), stop=(k == ext - 1)), "pe")
                            if k == 0:
                                ctl.wait(pe, "act", norm["tcp"])
                            den_t[k] = ctl.inc(pe.matmul(
                                pDen[:, :], ones_b[:, :], attb[k % 3],
                                start=(k == 0), stop=(k == ext - 1)), "pe")

                        for kt in range(ext):
                            par = kt % 2
                            if kt >= 2:
                                ctl.wait(pe, "act", exp_t[kt - 2])
                            else:
                                ctl.wait(pe, "act", prev_exp_last)
                            if kt == 0:
                                ctl.wait(pe, "act", tq0 if qbi == 0 else tq1)
                            tsm = ctl.inc(pe.matmul(
                                pS[par][:, :],
                                kT[g][:, kt * 128:(kt + 1) * 128],
                                qdst[:, qc:qc + 512],
                                start=True, stop=True), "pe")
                            if kt == 2:
                                flush_upto(qbc[0] - 3)
                                flush_one()
                            elif kt in (3, 4):
                                flush_one()
                            if kt >= 2:
                                emit_ctxden(kt - 2)
                            ctl.wait(act, "pe", tsm)
                            if kt >= 3:
                                ctl.wait(act, "pe", den_t[kt - 3])
                            exp_t[kt] = ctl.inc(act.activation(
                                attb[kt % 3], pS[par][:, :], AF.Exp,
                                bias=cpool[:, 1:2], scale=SCALE), "act")
                            if kt >= ext - 4:
                                i = kt - (ext - 4)
                                ctl.wait(dve, "act", exp_t[kt])
                                mask_t[kt] = ctl.inc(dve.tensor_mul(
                                    attb[kt % 3], attb[kt % 3],
                                    maskb[:, 384 - 128 * i: 896 - 128 * i]), "dve")
                        emit_ctxden(ext - 2)
                        emit_ctxden(ext - 1)
                        prev_exp_last = exp_t[ext - 1]
                        if qbi == 0 and hh < 3:
                            tq_next = emit_qproj(g * 4 + hh + 1, qh[(hh + 1) % 2])

                        cell = {}

                        def st_ln(tden=den_t[ext - 1], cell=cell):
                            ctl.wait(act, "pe", tden)
                            ctl.wait(act, "dve", max(norm["bank"]))
                            cell["ln"] = ctl.inc(act.activation(
                                bcf, pDen[:, :], AF.Ln, bias=0.0, scale=1.0), "act")
                            norm["tcp"] = cell["ln"]

                        def st_exp(cell=cell):
                            ctl.wait(act, "act", cell["ln"])
                            cell["rc"] = ctl.inc(act.activation(
                                bcf, bcf, AF.Exp, bias=0.0, scale=-1.0), "act")

                        def st_norm(tctx_=ctx_t[ext - 1], hh=hh, qc=qc, pCD=pCD, cell=cell):
                            ctl.wait(dve, "act", cell["rc"])
                            ctl.wait(dve, "pe", tctx_)
                            norm["bank"][pCD is pD] = ctl.inc(dve.tensor_mul(
                                ctxg[hh][:, qc:qc + 512], pCD[:, :], bcf), "dve")
                        pending.extend([(qbc[0] - 1, st_ln), (qbc[0] - 1, st_exp),
                                        (qbc[0] - 1, st_norm)])
                flush_pending()

                attn_done_dve = ctl.counts["dve"]
                attn_done_act = ctl.counts["act"]
                psw_free = [None, None, None, None]
                psw_banks = [pC, pD, pS[0], pS[1]]
                first_wo = True
                for o2 in range(8):
                    rslot, tl = ring_load(wo_r.ap()[g, o2, :, :, :, :])
                    ctl.wait(pe, RSEMS[rslot], tl)
                    for j in range(2):
                        o = o2 * 2 + j
                        for s in range(2):
                            pw = psw_banks[j * 2 + s]
                            pf = psw_free[j * 2 + s]
                            if pf is not None:
                                ctl.wait(pe, pf[0], pf[1])
                            if first_wo:
                                ctl.wait(pe, "dve", attn_done_dve)
                                ctl.wait(pe, "act", attn_done_act)
                                first_wo = False
                            tmm = 0
                            for co in range(4):
                                tmm = ctl.inc(pe.matmul(
                                    pw[:, :],
                                    ring[rslot][:, (j * 4 + co) * 128:(j * 4 + co + 1) * 128],
                                    ctxg[co][:, s * 512:(s + 1) * 512],
                                    start=(co == 0), stop=(co == 3)), "pe")
                            ctl.wait(dve, "pe", tmm)
                            te = ctl.inc(dve.tensor_add(
                                acc[o][:, s * 512:(s + 1) * 512],
                                acc[o][:, s * 512:(s + 1) * 512], pw[:, :]), "dve")
                            psw_free[j * 2 + s] = ("dve", te)
                    ring_done(rslot)

        snap = ctl.snapshot()
        ends = []
        ring_ends = []
        for i in nc.Switch(engines=sw_engines, index=sw_idx, n=2):
            ctl.restore(snap)
            ring_state["next"] = ring_snap["next"]
            ring_state["free"] = list(ring_snap["free"])
            body(i)
            ends.append(ctl.snapshot())
            ring_ends.append((ring_state["next"], list(ring_state["free"])))
        assert ends[0] == ends[1], f"switch unbalanced: {ends[0]} vs {ends[1]}"
        assert ring_ends[0] == ring_ends[1]
        ctl.restore(ends[0])

        if stop_after == "attn0":
            nc.all_engine_barrier()
            for i, nme in ((0, "acc0"), (8, "acc8")):
                dump(nme, acc[i][:, :])
            sy.wait_ge(ctl.sems["sq"], ctl.counts["sq"])
            nc.finalize()
            return nc, es

        ctl.wait(pe, "dve", ctl.counts["dve"])
        ctl.wait(pe, "act", ctl.counts["act"])
        ctl.wait(act, "pe", ctl.counts["pe"])
        ctl.wait(act, "dve", ctl.counts["dve"])
        xslots = [hg[0].bitcast(F32)[:, 0:512], hg[1].bitcast(F32)[:, 0:512]]
        LD = ["qo0", "qo1"]
        ST = ["sta", "stb"]
        st_free = [0, 0]
        tr_free = [0, 0]
        trb = [pA, pB]
        for qi in range(32):
            tile, q4 = qi // 4, qi % 4
            r0 = p * 1024 + tile * 128
            slot = qi % 2
            bank = qi % 2
            ctl.wait(act, ST[slot], st_free[slot])
            tl = ctl.dinc(act.dma_start(
                out=xslots[slot],
                in_=x_own.ap()[r0:r0 + 128, q4 * 512:(q4 + 1) * 512]), LD[slot])
            ctl.wait(pe, "dve", tr_free[bank])
            tp = 0
            for f in range(4):
                fg = q4 * 4 + f
                tp = ctl.inc(pe.transpose(trb[bank][:, f * 128:(f + 1) * 128],
                                          acc[fg][:, tile * 128:(tile + 1) * 128],
                                          ident_f[:, :]), "pe")
            ctl.wait(dve, "pe", tp)
            ctl.wait(dve, LD[slot], tl)
            ta = ctl.inc(dve.tensor_add(xslots[slot], xslots[slot],
                                        trb[bank][:, :]), "dve")
            tr_free[bank] = ta
            ctl.wait(sy, "dve", ta)
            st_free[slot] = ctl.dinc(sy.dma_start(
                out=out.ap()[r0:r0 + 128, q4 * 512:(q4 + 1) * 512],
                in_=xslots[slot]), ST[slot])

    for s_ in ("sta", "stb", "stc", "std"):
        sy.wait_ge(ctl.sems[s_], ctl.counts[s_])
    sy.wait_ge(ctl.sems["sq"], ctl.counts["sq"])
    nc.finalize()
    return nc, es



def _prep(inputs):
    x = np.asarray(inputs["L0_attn_x"], dtype=np.float32)
    Wq = np.asarray(inputs["L0_attn_Wq"], dtype=np.float32)
    Wk = np.asarray(inputs["L0_attn_Wk"], dtype=np.float32)
    Wv = np.asarray(inputs["L0_attn_Wv"], dtype=np.float32)
    Wo = np.asarray(inputs["L0_attn_Wo"], dtype=np.float32)
    bo = np.asarray(inputs["L0_attn_bo"], dtype=np.float32)
    wgu = np.asarray(inputs["L0_ffn_w_gate_up"], dtype=np.float32)
    wdn = np.asarray(inputs["L0_ffn_w_down"], dtype=np.float32)

    wq_r = np.ascontiguousarray(Wq.reshape(DCH, 128, NH, 128).transpose(2, 1, 0, 3))
    wgu_r = np.ascontiguousarray(
        wgu.reshape(128, 128, DCH, 128).transpose(0, 3, 2, 1))
    wdn_r = np.ascontiguousarray(
        wdn.reshape(16, 128, 8, 8, 128).transpose(2, 0, 4, 3, 1))
    wo_r = np.ascontiguousarray(
        Wo.reshape(G, 4, 128, 8, 2, 128).transpose(0, 3, 2, 4, 1, 5))
    bo_r = np.ascontiguousarray(bo.reshape(16, 128).T)

    idents = np.eye(128, dtype=np.float32)
    identb = np.eye(128, dtype=ml_dtypes.bfloat16)
    ii = np.arange(128)[:, None]
    jj = np.arange(896)[None, :]
    maskbig = (ii <= jj - 384).astype(ml_dtypes.bfloat16)

    in_maps = []
    for core in range(8):
        b, pair = core // 2, core % 2
        rows = []
        for (pa, pb) in OWN_BLOCKS[pair]:
            rows.append(x[b, pa * 512:(pa + 1) * 512])
            rows.append(x[b, pb * 512:(pb + 1) * 512])
        x_own = np.ascontiguousarray(np.concatenate(rows, axis=0))
        in_maps.append({
            "x_ext": np.ascontiguousarray(x[b]),
            "x_own": x_own,
            "wq_r": wq_r, "wk": Wk, "wv": Wv, "wo_r": wo_r, "bo_r": bo_r,
            "wgu_r": wgu_r, "wdn_r": wdn_r,
            "idents": idents, "identb": identb, "maskbig": maskbig,
        })
    return in_maps


def kernel(**inputs):
    in_maps = _prep(inputs)
    nc, es = build_program()
    res = run_bass_kernel_spmd(nc, in_maps, core_ids=list(range(8)))
    out = np.zeros((B, S, D), dtype=np.float32)
    for core in range(8):
        b, pair = core // 2, core % 2
        o = res.results[core]["out"]
        r = 0
        for (pa, pb) in OWN_BLOCKS[pair]:
            out[b, pa * 512:(pa + 1) * 512] = o[r:r + 512]
            out[b, pb * 512:(pb + 1) * 512] = o[r + 512:r + 1024]
            r += 1024
    return out


# version 14
# speedup vs baseline: 1.0210x; 1.0210x over previous
import contextlib
import numpy as np
import ml_dtypes

import concourse.bass as bass
import concourse.mybir as mybir
from concourse.bass_utils import run_bass_kernel_spmd

F32 = mybir.dt.float32
BF16 = mybir.dt.bfloat16
AF = mybir.ActivationFunctionType
ALU = mybir.AluOpType

D = 2048
S = 4096
B = 4
NH = 16
G = 4
NKT = 32
DCH = 16
SCALE = float(1.0 / np.sqrt(128.0))
SHIFT = -12.0
OWN_BLOCKS = {0: [(0, 7), (1, 6)], 1: [(2, 5), (3, 4)]}


class Ctl:
    DMA_SEMS = ("sq", "gq", "xta", "xtb", "gu0", "gu1", "dn0", "dn1",
                "qo0", "qo1", "sta", "stb", "stc", "std",
                "pq0", "pq1", "pq2", "pq3")

    def __init__(self, nc, es):
        self.sems = {}
        self.counts = {}
        for name in ("pe", "act", "dve", "pool") + self.DMA_SEMS:
            self.sems[name] = es.enter_context(nc.semaphore("s_" + name))
            self.counts[name] = 0

    def inc(self, instr, name, n=1):
        self.counts[name] += n
        instr.then_inc(self.sems[name], n)
        return self.counts[name]

    def dinc(self, instr, name):
        return self.inc(instr, name, 16)

    def wait(self, eng, name, val):
        if val and val > 0:
            eng.wait_ge(self.sems[name], val)

    def snapshot(self):
        return dict(self.counts)

    def restore(self, snap):
        self.counts = dict(snap)


def build_program(stop_after=None, dumps=()):
    nc = bass.Bass("TRN2", target_bir_lowering=False)
    es = contextlib.ExitStack()

    x_ext = nc.dram_tensor("x_ext", [S, D], F32, kind="ExternalInput")
    x_own = nc.dram_tensor("x_own", [2048, D], F32, kind="ExternalInput")
    wq_r = nc.dram_tensor("wq_r", [NH, 128, DCH, 128], F32, kind="ExternalInput")
    wk_n = nc.dram_tensor("wk", [D, 512], F32, kind="ExternalInput")
    wv_n = nc.dram_tensor("wv", [D, 512], F32, kind="ExternalInput")
    wo_r = nc.dram_tensor("wo_r", [G, 8, 128, 2, 4, 128], F32, kind="ExternalInput")
    bo_r = nc.dram_tensor("bo_r", [128, 16], F32, kind="ExternalInput")
    wgu_r = nc.dram_tensor("wgu_r", [128, 128, DCH, 128], F32, kind="ExternalInput")
    wdn_r = nc.dram_tensor("wdn_r", [8, 16, 128, 8, 128], F32, kind="ExternalInput")
    idents = nc.dram_tensor("idents", [128, 128], F32, kind="ExternalInput")
    identb = nc.dram_tensor("identb", [128, 128], BF16, kind="ExternalInput")
    maskbig = nc.dram_tensor("maskbig", [128, 896], BF16, kind="ExternalInput")
    out = nc.dram_tensor("out", [2048, D], F32, kind="ExternalOutput")

    dbg = {}
    for name, cols, dt in dumps:
        dbg[name] = nc.dram_tensor("dbg_" + name, [128, cols], dt, kind="ExternalOutput")

    ctl = Ctl(nc, es)
    pe, act, dve, sy, gq = nc.tensor, nc.scalar, nc.vector, nc.sync, nc.gpsimd

    sbuf = lambda n, sh, dt: es.enter_context(nc.sbuf_tensor(n, sh, dt))
    psum = lambda n, sh, dt: es.enter_context(nc.psum_tensor(n, sh, dt))

    kT = [sbuf(f"kT{g}", [128, NKT * 128], BF16) for g in range(G)]
    vT = [sbuf(f"vT{g}", [128, NKT * 128], BF16) for g in range(G)]
    ident_f = sbuf("ident_f", [128, 128], F32)
    maskb = sbuf("maskb", [128, 896], BF16)
    bo_s = sbuf("bo_s", [128, 16], F32)
    ones_b = sbuf("ones_b", [128, 128], BF16)
    stat = sbuf("stat", [128, 32], F32)
    cpool = sbuf("cpool", [128, 4], F32)
    xnl = sbuf("xnl", [128, 1024], F32)
    xt = [sbuf(f"xt{i}", [128, 1024], F32) for i in range(2)]

    pA = psum("pA", [128, 512], F32)
    pB = psum("pB", [128, 512], F32)
    pC = psum("pC", [128, 512], F32)
    pD = psum("pD", [128, 512], F32)
    pS = [psum(f"pS{i}", [128, 512], F32) for i in range(2)]
    pDen = psum("pDen", [128, 512], F32)


    def dump(name, src_ap):
        if name in dbg:
            t = ctl.dinc(sy.dma_start(out=dbg[name].ap()[:, :], in_=src_ap), "sq")
            sy.wait_ge(ctl.sems["sq"], t)

    ctl.dinc(sy.dma_start(out=ident_f[:, :], in_=idents.ap()[:, :]), "sq")
    ctl.dinc(sy.dma_start(out=maskb[:, :], in_=maskbig.ap()[:, :]), "sq")
    ctl.dinc(sy.dma_start(out=bo_s[:, :], in_=bo_r.ap()[:, :]), "sq")
    dve.memset(ones_b[:, :], 1.0)
    dve.memset(cpool[:, 0:1], 1e-5)
    dve.memset(cpool[:, 1:2], SHIFT)
    sy.wait_ge(ctl.sems["sq"], ctl.counts["sq"])
    nc.all_engine_barrier()

    sw_engines = [pe, act, dve, gq]
    sw_idx = [e.compute_val(e.partition_id() % 2) for e in sw_engines]

    pt_evac = [0, 0]

    def emit_ln_tile(x_dram, r0, dest3d, prev, bufs=None):
        bxt, bstat, bxnl, lds = bufs if bufs is not None else (xt, stat, xnl, ("xta", "xtb"))
        ctl.wait(sy, "act", prev.get("xapply", 0))
        t0 = ctl.dinc(sy.dma_start(out=bxt[0][:, :],
                                   in_=x_dram.ap()[r0:r0 + 128, 0:1024]), lds[0])
        t1 = ctl.dinc(sy.dma_start(out=bxt[1][:, :],
                                   in_=x_dram.ap()[r0:r0 + 128, 1024:2048]), lds[1])
        ctl.wait(dve, "act", prev.get("xapply", 0))
        ctl.wait(dve, lds[0], t0)
        dve.bn_stats(bstat[:, 0:6], bxt[0][:, 0:512])
        dve.bn_stats(bstat[:, 6:12], bxt[0][:, 512:1024])
        ctl.wait(dve, lds[1], t1)
        dve.bn_stats(bstat[:, 12:18], bxt[1][:, 0:512])
        tb = ctl.inc(dve.bn_stats(bstat[:, 18:24], bxt[1][:, 512:1024]), "dve")
        ctl.wait(dve, "dve", tb)
        ta = ctl.inc(dve.bn_aggr(bstat[:, 24:26], bstat[:, 0:24]), "dve")
        ctl.wait(act, "dve", ta)
        ts = ctl.inc(act.activation(bstat[:, 26:27], bstat[:, 25:26], AF.Sqrt,
                                    bias=cpool[:, 0:1], scale=1.0), "act")
        ctl.wait(dve, "act", ts)
        tr = ctl.inc(dve.reciprocal(bstat[:, 27:28], bstat[:, 26:27]), "dve")
        ctl.wait(dve, "dve", tr)
        tn = ctl.inc(dve.scalar_tensor_tensor(bstat[:, 28:29], bstat[:, 24:25], -1.0,
                                              bstat[:, 27:28], ALU.mult, ALU.mult), "dve")
        ctl.wait(act, "dve", tn)
        tp = prev.get("transpose", 0)
        tx = 0
        for half in range(2):
            ctl.wait(act, "pe", tp)
            tx = ctl.inc(act.activation(bxnl[:, :], bxt[half][:, :], AF.Identity,
                                        bias=bstat[:, 28:29], scale=bstat[:, 27:28]), "act")
            for q4 in range(2):
                qg = half * 2 + q4
                pq = qg % 2
                pt_bank = pS[pq]
                ctl.wait(pe, "act", tx)
                ctl.wait(pe, "act", pt_evac[pq])
                for j in range(4):
                    dcl = q4 * 4 + j
                    tp = ctl.inc(pe.transpose(pt_bank[:, j * 128:(j + 1) * 128],
                                              bxnl[:, dcl * 128:(dcl + 1) * 128],
                                              ident_f[:, :]), "pe")
                ctl.wait(act, "pe", tp)
                pt_evac[pq] = ctl.inc(act.copy(
                    dest3d(qg),
                    pt_bank[:, :].rearrange("p (a b) -> p a b", a=4)), "act")
        return {"xapply": tx, "transpose": tp, "evac": max(pt_evac)}

    xn = sbuf("xn", [128, DCH * 1024], BF16)

    with contextlib.ExitStack() as p1es:
        wkb = p1es.enter_context(nc.sbuf_tensor("wkb", [128, DCH * 512], BF16))
        wvb = p1es.enter_context(nc.sbuf_tensor("wvb", [128, DCH * 512], BF16))
        xnT = p1es.enter_context(nc.sbuf_tensor("xnT", [128, DCH * 512], BF16))
        xt2 = [p1es.enter_context(nc.sbuf_tensor(f"xt2_{i}", [128, 1024], F32))
               for i in range(2)]
        stat2 = p1es.enter_context(nc.sbuf_tensor("stat2", [128, 32], F32))
        xnl2 = p1es.enter_context(nc.sbuf_tensor("xnl2", [128, 1024], F32))
        xt3 = [p1es.enter_context(nc.sbuf_tensor(f"xt3_{i}", [128, 1024], F32))
               for i in range(2)]
        stat3 = p1es.enter_context(nc.sbuf_tensor("stat3", [128, 32], F32))
        xnl3 = p1es.enter_context(nc.sbuf_tensor("xnl3", [128, 1024], F32))
        scr = p1es.enter_context(nc.sbuf_tensor("scr", [128, 192], BF16))
        scr_init = ctl.inc(gq.memset(scr[:, :], 0.0), "pool")
        bufsets = [(xt, stat, xnl, ("xta", "xtb")),
                   (xt2, stat2, xnl2, ("qo0", "qo1")),
                   (xt3, stat3, xnl3, ("stc", "std"))]
        xnT3 = xnT[:, :].rearrange("p (d c) -> p d c", d=DCH)
        tw = 0
        for d in range(DCH):
            ctl.dinc(gq.dma_start(out=wkb[:, d * 512:(d + 1) * 512],
                                  in_=wk_n.ap()[d * 128:(d + 1) * 128, :]), "gq")
            tw = ctl.dinc(gq.dma_start(out=wvb[:, d * 512:(d + 1) * 512],
                                       in_=wv_n.ap()[d * 128:(d + 1) * 128, :]), "gq")

        prevs = [{}, {}, {}]
        k_free = {0: 0, 1: 0}
        v_free = {0: 0, 1: 0}
        ctl.wait(pe, "gq", tw)
        for st in range(8):
            for tt in range(4):
                r0 = st * 512 + tt * 128
                bs = (st * 4 + tt) % 3

                def dest3d(q4, tt=tt):
                    base = xnT[:, :].rearrange("p (d c) -> p d c", d=DCH)
                    return base[:, q4 * 4:(q4 + 1) * 4, tt * 128:(tt + 1) * 128]

                prevs[bs] = emit_ln_tile(x_ext, r0, dest3d, prevs[bs],
                                         bufsets[bs])
            prev = {"evac": max(pv.get("evac", 0) for pv in prevs)}
            own_map = {0: (0, 0), 2: (1, 0), 5: (1, 1), 7: (0, 1)}
            if st in own_map:
                own_pair, half = own_map[st]
                snap = ctl.snapshot()
                ends = []
                for i in nc.Switch(engines=sw_engines, index=sw_idx, n=2):
                    ctl.restore(snap)
                    if i == own_pair:
                        ctl.wait(gq, "act", prev["evac"])
                        for d_ in range(DCH):
                            ctl.inc(gq.tensor_copy(
                                xn[:, d_ * 1024 + half * 512: d_ * 1024 + (half + 1) * 512],
                                xnT[:, d_ * 512:(d_ + 1) * 512]), "pool")
                    else:
                        ctl.wait(gq, "pool", scr_init)
                        base_slot = {0: 0, 2: 16, 5: 32, 7: 48}[st]
                        for d_ in range(DCH):
                            sl = (base_slot + d_) * 2
                            ctl.inc(gq.tensor_copy(scr[:, sl:sl + 2],
                                                   scr[:, 128:130]), "pool")
                    ends.append(ctl.snapshot())
                assert ends[0] == ends[1]
                ctl.restore(ends[0])
                ctl.wait(act, "pool", ctl.counts["pool"])
            ctl.wait(pe, "act", prev["evac"])
            for g in range(G):
                bank = g % 2
                pk = pA if bank == 0 else pB
                ctl.wait(pe, "act", k_free[bank])
                tm = 0
                for d in range(DCH):
                    tm = ctl.inc(pe.matmul(pk[:, :],
                                           wkb[:, d * 512 + g * 128: d * 512 + (g + 1) * 128],
                                           xnT[:, d * 512:(d + 1) * 512],
                                           start=(d == 0), stop=(d == DCH - 1)), "pe")
                ctl.wait(act, "pe", tm)
                k_free[bank] = ctl.inc(act.copy(kT[g][:, st * 512:(st + 1) * 512], pk[:, :]), "act")
            for tt in range(4):
                bank = tt % 2
                pv = pC if bank == 0 else pD
                ctl.wait(pe, "act", v_free[bank])
                tm = 0
                for d in range(DCH):
                    tm = ctl.inc(pe.matmul(pv[:, :],
                                           xnT[:, d * 512 + tt * 128: d * 512 + (tt + 1) * 128],
                                           wvb[:, d * 512:(d + 1) * 512],
                                           start=(d == 0), stop=(d == DCH - 1)), "pe")
                ctl.wait(act, "pe", tm)
                tv = 0
                kt_idx = st * 4 + tt
                for g in range(G):
                    tv = ctl.inc(act.copy(vT[g][:, kt_idx * 128:(kt_idx + 1) * 128],
                                          pv[:, g * 128:(g + 1) * 128]), "act")
                v_free[bank] = tv

        p1_prev0 = prevs[0]

    dump("kT0", kT[0][:, :])
    dump("vT0", vT[0][:, :])
    pe_after_p1 = ctl.counts["pe"]
    act_after_p1 = ctl.counts["act"]
    if stop_after == "p1":
        nc.all_engine_barrier()
        sy.wait_ge(ctl.sems["sq"], ctl.counts["sq"])
        nc.finalize()
        return nc, es

    acc = [sbuf(f"acc{d}", [128, 1024], F32) for d in range(DCH)]
    hg = [sbuf(f"hg{c}", [128, 1024], BF16) for c in range(8)]
    ctxg = [sbuf(f"ctx{h}", [128, 1024], BF16) for h in range(4)]
    qh0 = None
    ring = [sbuf(f"wr{i}", [128, 1024], BF16) for i in range(4)]
    RSEMS = ["gu0", "gu1", "dn0", "dn1"]
    ring_state = {"next": 0, "free": [pe_after_p1] * 4}

    def ring_load(src_ap):
        i = ring_state["next"]
        ring_state["next"] = (i + 1) % len(ring)
        ctl.wait(gq, "pe", ring_state["free"][i])
        tl = ctl.dinc(gq.dma_start(out=ring[i][:, :], in_=src_ap), RSEMS[i])
        return i, tl

    def ring_done(i):
        ring_state["free"][i] = ctl.counts["pe"]

    attb = [hg[0][:, 0:512], hg[0][:, 512:1024], hg[2][:, 0:512], hg[2][:, 512:1024]]
    qh = [hg[1][:, :], hg[3][:, :]]
    bcf = xt[1][:, 512:1024]
    rd = xt[1][0:1, 0:512]

    def xn_s(d, s):
        return xn[:, d * 1024 + s * 512: d * 1024 + (s + 1) * 512]

    for p in range(2):
        prev = p1_prev0 if p == 0 else {}
        if p > 0:
            for tile in range(8):
                r0 = p * 1024 + tile * 128

                def dest3d(q4, tile=tile):
                    base = xn[:, :].rearrange("p (d c) -> p d c", d=DCH)
                    return base[:, q4 * 4:(q4 + 1) * 4, tile * 128:(tile + 1) * 128]

                prev = emit_ln_tile(x_own, r0, dest3d, prev)
        ln_evac_last = prev.get("evac", 0)
        if p == 0:
            ctl.wait(pe, "pool", ctl.counts["pool"])
        if stop_after == "ln0":
            nc.all_engine_barrier()
            dump("xn", xn[:, 0:4096])
            sy.wait_ge(ctl.sems["sq"], ctl.counts["sq"])
            nc.finalize()
            return nc, es

        h_war = 0
        psd_free = [None, None, None, None]
        psd_banks = [pC, pD, pS[0], pS[1]]
        h_tok = {}
        ctl.wait(pe, "act", max(ln_evac_last, act_after_p1))
        for grp in range(8):
            prev_h = {}
            for c8 in range(8):
                c = grp * 8 + c8
                for which, blk in (("g", c), ("u", 64 + c)):
                    loads = [ring_load(wgu_r.ap()[blk, :, hf * 8:(hf + 1) * 8, :])
                             for hf in range(2)]
                    tmm_s = {0: 0, 1: 0}
                    for half in range(2):
                        slot, tl = loads[half]
                        ctl.wait(pe, RSEMS[slot], tl)
                        for s in range(2):
                            pg = pA if s == 0 else pB
                            if half == 0:
                                if which == "g":
                                    ctl.wait(pe, "dve", prev_h.get(s, 0))
                                else:
                                    ctl.wait(pe, "act", h_tok.get(("silu", c8, s), 0))
                            for j in range(8):
                                d = half * 8 + j
                                tmm_s[s] = ctl.inc(pe.matmul(
                                    pg[:, :],
                                    ring[slot][:, j * 128:(j + 1) * 128],
                                    xn_s(d, s),
                                    start=(d == 0), stop=(d == DCH - 1)), "pe")
                        ring_done(slot)
                    for s in range(2):
                        pg = pA if s == 0 else pB
                        if which == "g":
                            ctl.wait(act, "pe", tmm_s[s])
                            h_tok[("silu", c8, s)] = ctl.inc(act.activation(
                                hg[c8][:, s * 512:(s + 1) * 512], pg[:, :],
                                AF.Silu, bias=0.0, scale=1.0), "act")
                        else:
                            ctl.wait(dve, "pe", tmm_s[s])
                            ctl.wait(dve, "act", h_tok[("silu", c8, s)])
                            ctl.wait(dve, "pe", h_war)
                            h_tok[(c8, s)] = ctl.inc(dve.tensor_mul(
                                hg[c8][:, s * 512:(s + 1) * 512],
                                hg[c8][:, s * 512:(s + 1) * 512],
                                pg[:, :]), "dve")
                prev_h = {0: h_tok[(c8, 0)], 1: h_tok[(c8, 1)]}
            for o in range(16):
                rslot, tl = ring_load(wdn_r.ap()[grp, o, :, :, :])
                bank = o % 2
                ctl.wait(pe, RSEMS[rslot], tl)
                for s in range(2):
                    pd = psd_banks[bank * 2 + s]
                    pf = psd_free[bank * 2 + s]
                    if pf is not None:
                        ctl.wait(pe, pf[0], pf[1])
                    tmm = 0
                    for c8 in range(8):
                        ctl.wait(pe, "dve", h_tok[(c8, s)])
                        tmm = ctl.inc(pe.matmul(
                            pd[:, :],
                            ring[rslot][:, c8 * 128:(c8 + 1) * 128],
                            hg[c8][:, s * 512:(s + 1) * 512],
                            start=(c8 == 0), stop=(c8 == 7)), "pe")
                    if grp == 0:
                        ctl.wait(act, "pe", tmm)
                        te = ctl.inc(act.activation(
                            acc[o][:, s * 512:(s + 1) * 512], pd[:, :],
                            AF.Identity, bias=bo_s[:, o:o + 1], scale=1.0), "act")
                        psd_free[bank * 2 + s] = ("act", te)
                    else:
                        ctl.wait(dve, "pe", tmm)
                        te = ctl.inc(dve.tensor_add(
                            acc[o][:, s * 512:(s + 1) * 512],
                            acc[o][:, s * 512:(s + 1) * 512], pd[:, :]), "dve")
                        psd_free[bank * 2 + s] = ("dve", te)
                ring_done(rslot)
            h_war = ctl.counts["pe"]
        if stop_after == "ffn0":
            nc.all_engine_barrier()
            for i, nme in ((0, "acc0"), (8, "acc8")):
                dump(nme, acc[i][:, :])
            sy.wait_ge(ctl.sems["sq"], ctl.counts["sq"])
            nc.finalize()
            return nc, es

        p_exts = {0: [4 * (OWN_BLOCKS[0][p][0] + 1), 4 * (OWN_BLOCKS[0][p][1] + 1)],
                  1: [4 * (OWN_BLOCKS[1][p][0] + 1), 4 * (OWN_BLOCKS[1][p][1] + 1)]}

        dve_at_body_start = ctl.counts["dve"]
        act_at_body_start = ctl.counts["act"]
        ring_snap = dict(ring_state, free=list(ring_state["free"]))

        def body(pair):
            e_a, e_b = p_exts[pair]
            for g in range(G):
                ctl.wait(pe, "dve", dve_at_body_start if g == 0 else ctl.counts["dve"])
                ctl.wait(pe, "act", act_at_body_start if g == 0 else ctl.counts["act"])
                prev_q = [0, 0]
                norm = {"bank": [0, 0], "tcp": 0}
                qbc = [0]
                prev_exp_last = 0
                pending = []

                def flush_upto(qmax):
                    while pending and pending[0][0] <= qmax:
                        pending.pop(0)[1]()

                def flush_one():
                    if pending:
                        pending.pop(0)[1]()

                def flush_pending():
                    while pending:
                        pending.pop(0)[1]()

                def emit_qproj(h, qdst):
                    nonlocal prev_q
                    tmm_q = 0
                    loads = [ring_load(wq_r.ap()[h, :, hf * 8:(hf + 1) * 8, :])
                             for hf in range(2)]
                    for hf in range(2):
                        slot, tl = loads[hf]
                        ctl.wait(pe, RSEMS[slot], tl)
                        if hf == 0:
                            ctl.wait(pe, "act", prev_q[1])
                        for s in range(2):
                            pq = pA if s == 0 else pB
                            for j in range(8):
                                d = hf * 8 + j
                                tmm_q = ctl.inc(pe.matmul(
                                    pq[:, :],
                                    ring[slot][:, j * 128:(j + 1) * 128],
                                    xn_s(d, s),
                                    start=(d == 0), stop=(d == DCH - 1)), "pe")
                        ring_done(slot)
                    ctl.wait(act, "pe", tmm_q)
                    tq0 = ctl.inc(act.copy(qdst[:, 0:512], pA[:, :]), "act")
                    tq1 = ctl.inc(act.copy(qdst[:, 512:1024], pB[:, :]), "act")
                    prev_q = [tq0, tq1]
                    return tq0, tq1

                tq_next = emit_qproj(g * 4, qh[0])
                for hh in range(4):
                    h = g * 4 + hh
                    qdst = qh[hh % 2]
                    tq0, tq1 = tq_next
                    for qbi, ext in ((0, e_a), (1, e_b)):
                        qc = qbi * 512
                        exp_t = {}
                        mask_t = {}
                        ctx_t = {}
                        den_t = {}
                        pCD = [pC, pD][qbc[0] % 2]
                        qbc[0] += 1

                        def emit_ctxden(k, ext=ext, g=g, pCD=pCD):
                            if k == 0:
                                ctl.wait(pe, "dve", norm["bank"][pCD is pD])
                            if k in mask_t:
                                ctl.wait(pe, "dve", mask_t[k])
                            else:
                                ctl.wait(pe, "act", exp_t[k])
                            ctx_t[k] = ctl.inc(pe.matmul(
                                pCD[:, :],
                                vT[g][:, k * 128:(k + 1) * 128],
                                attb[k % 4],
                                start=(k == 0), stop=(k == ext - 1)), "pe")
                            if k == 0:
                                ctl.wait(pe, "act", norm["tcp"])
                            den_t[k] = ctl.inc(pe.matmul(
                                pDen[:, :], ones_b[:, :], attb[k % 4],
                                start=(k == 0), stop=(k == ext - 1)), "pe")

                        for kt in range(ext):
                            par = kt % 2
                            if kt >= 2:
                                ctl.wait(pe, "act", exp_t[kt - 2])
                            else:
                                ctl.wait(pe, "act", prev_exp_last)
                            if kt == 0:
                                ctl.wait(pe, "act", tq0 if qbi == 0 else tq1)
                            tsm = ctl.inc(pe.matmul(
                                pS[par][:, :],
                                kT[g][:, kt * 128:(kt + 1) * 128],
                                qdst[:, qc:qc + 512],
                                start=True, stop=True), "pe")
                            if kt == 2:
                                flush_upto(qbc[0] - 3)
                                flush_one()
                            elif kt in (3, 4):
                                flush_one()
                            if kt >= 3:
                                emit_ctxden(kt - 3)
                            ctl.wait(act, "pe", tsm)
                            if kt >= 4:
                                ctl.wait(act, "pe", den_t[kt - 4])
                            exp_t[kt] = ctl.inc(act.activation(
                                attb[kt % 4], pS[par][:, :], AF.Exp,
                                bias=cpool[:, 1:2], scale=SCALE), "act")
                            if kt >= ext - 4:
                                i = kt - (ext - 4)
                                ctl.wait(dve, "act", exp_t[kt])
                                mask_t[kt] = ctl.inc(dve.tensor_mul(
                                    attb[kt % 4], attb[kt % 4],
                                    maskb[:, 384 - 128 * i: 896 - 128 * i]), "dve")
                        emit_ctxden(ext - 3)
                        emit_ctxden(ext - 2)
                        emit_ctxden(ext - 1)
                        prev_exp_last = exp_t[ext - 1]
                        if qbi == 0 and hh < 3:
                            tq_next = emit_qproj(g * 4 + hh + 1, qh[(hh + 1) % 2])

                        cell = {}

                        def st_ln(tden=den_t[ext - 1], cell=cell):
                            ctl.wait(act, "pe", tden)
                            ctl.wait(act, "dve", max(norm["bank"]))
                            cell["ln"] = ctl.inc(act.activation(
                                bcf, pDen[:, :], AF.Ln, bias=0.0, scale=1.0), "act")
                            norm["tcp"] = cell["ln"]

                        def st_exp(cell=cell):
                            ctl.wait(act, "act", cell["ln"])
                            cell["rc"] = ctl.inc(act.activation(
                                bcf, bcf, AF.Exp, bias=0.0, scale=-1.0), "act")

                        def st_norm(tctx_=ctx_t[ext - 1], hh=hh, qc=qc, pCD=pCD, cell=cell):
                            ctl.wait(dve, "act", cell["rc"])
                            ctl.wait(dve, "pe", tctx_)
                            norm["bank"][pCD is pD] = ctl.inc(dve.tensor_mul(
                                ctxg[hh][:, qc:qc + 512], pCD[:, :], bcf), "dve")
                        pending.extend([(qbc[0] - 1, st_ln), (qbc[0] - 1, st_exp),
                                        (qbc[0] - 1, st_norm)])
                flush_pending()

                attn_done_dve = ctl.counts["dve"]
                attn_done_act = ctl.counts["act"]
                psw_free = [None, None, None, None]
                psw_banks = [pC, pD, pS[0], pS[1]]
                first_wo = True
                for o2 in range(8):
                    rslot, tl = ring_load(wo_r.ap()[g, o2, :, :, :, :])
                    ctl.wait(pe, RSEMS[rslot], tl)
                    for j in range(2):
                        o = o2 * 2 + j
                        for s in range(2):
                            pw = psw_banks[j * 2 + s]
                            pf = psw_free[j * 2 + s]
                            if pf is not None:
                                ctl.wait(pe, pf[0], pf[1])
                            if first_wo:
                                ctl.wait(pe, "dve", attn_done_dve)
                                ctl.wait(pe, "act", attn_done_act)
                                first_wo = False
                            tmm = 0
                            for co in range(4):
                                tmm = ctl.inc(pe.matmul(
                                    pw[:, :],
                                    ring[rslot][:, (j * 4 + co) * 128:(j * 4 + co + 1) * 128],
                                    ctxg[co][:, s * 512:(s + 1) * 512],
                                    start=(co == 0), stop=(co == 3)), "pe")
                            ctl.wait(dve, "pe", tmm)
                            te = ctl.inc(dve.tensor_add(
                                acc[o][:, s * 512:(s + 1) * 512],
                                acc[o][:, s * 512:(s + 1) * 512], pw[:, :]), "dve")
                            psw_free[j * 2 + s] = ("dve", te)
                    ring_done(rslot)

        snap = ctl.snapshot()
        ends = []
        ring_ends = []
        for i in nc.Switch(engines=sw_engines, index=sw_idx, n=2):
            ctl.restore(snap)
            ring_state["next"] = ring_snap["next"]
            ring_state["free"] = list(ring_snap["free"])
            body(i)
            ends.append(ctl.snapshot())
            ring_ends.append((ring_state["next"], list(ring_state["free"])))
        assert ends[0] == ends[1], f"switch unbalanced: {ends[0]} vs {ends[1]}"
        assert ring_ends[0] == ring_ends[1]
        ctl.restore(ends[0])

        if stop_after == "attn0":
            nc.all_engine_barrier()
            for i, nme in ((0, "acc0"), (8, "acc8")):
                dump(nme, acc[i][:, :])
            sy.wait_ge(ctl.sems["sq"], ctl.counts["sq"])
            nc.finalize()
            return nc, es

        ctl.wait(pe, "dve", ctl.counts["dve"])
        ctl.wait(pe, "act", ctl.counts["act"])
        ctl.wait(sy, "pe", ctl.counts["pe"])
        ctl.wait(sy, "dve", ctl.counts["dve"])
        ctl.wait(gq, "pe", ctl.counts["pe"])
        ctl.wait(gq, "dve", ctl.counts["dve"])
        xslots = [xt[0][:, 0:512], xt[0][:, 512:1024],
                  xt[1][:, 0:512], xt[1][:, 512:1024]]
        LD = ["pq0", "pq1", "pq2", "pq3"]
        ST = ["sta", "stb", "stc", "std"]
        st_free = [0, 0, 0, 0]
        tr_free = [0, 0]
        trb = [pA, pB]
        for qi in range(32):
            tile, q4 = qi // 4, qi % 4
            r0 = p * 1024 + tile * 128
            slot = qi % 4
            bank = qi % 2
            ctl.wait(gq, ST[slot], st_free[slot])
            tl = ctl.dinc(gq.dma_start(
                out=xslots[slot],
                in_=x_own.ap()[r0:r0 + 128, q4 * 512:(q4 + 1) * 512]), LD[slot])
            ctl.wait(pe, "dve", tr_free[bank])
            tp = 0
            for f in range(4):
                fg = q4 * 4 + f
                tp = ctl.inc(pe.transpose(trb[bank][:, f * 128:(f + 1) * 128],
                                          acc[fg][:, tile * 128:(tile + 1) * 128],
                                          ident_f[:, :]), "pe")
            ctl.wait(dve, "pe", tp)
            ctl.wait(dve, LD[slot], tl)
            ta = ctl.inc(dve.tensor_add(xslots[slot], xslots[slot],
                                        trb[bank][:, :]), "dve")
            tr_free[bank] = ta
            ctl.wait(sy, "dve", ta)
            st_free[slot] = ctl.dinc(sy.dma_start(
                out=out.ap()[r0:r0 + 128, q4 * 512:(q4 + 1) * 512],
                in_=xslots[slot]), ST[slot])
        for s_ in ("sta", "stb", "stc", "std"):
            sy.wait_ge(ctl.sems[s_], ctl.counts[s_])
        nc.all_engine_barrier()

    for s_ in ("sta", "stb", "stc", "std"):
        sy.wait_ge(ctl.sems[s_], ctl.counts[s_])
    sy.wait_ge(ctl.sems["sq"], ctl.counts["sq"])
    nc.finalize()
    return nc, es



def _prep(inputs):
    x = np.asarray(inputs["L0_attn_x"], dtype=np.float32)
    Wq = np.asarray(inputs["L0_attn_Wq"], dtype=np.float32)
    Wk = np.asarray(inputs["L0_attn_Wk"], dtype=np.float32)
    Wv = np.asarray(inputs["L0_attn_Wv"], dtype=np.float32)
    Wo = np.asarray(inputs["L0_attn_Wo"], dtype=np.float32)
    bo = np.asarray(inputs["L0_attn_bo"], dtype=np.float32)
    wgu = np.asarray(inputs["L0_ffn_w_gate_up"], dtype=np.float32)
    wdn = np.asarray(inputs["L0_ffn_w_down"], dtype=np.float32)

    wq_r = np.ascontiguousarray(Wq.reshape(DCH, 128, NH, 128).transpose(2, 1, 0, 3))
    wgu_r = np.ascontiguousarray(
        wgu.reshape(128, 128, DCH, 128).transpose(0, 3, 2, 1))
    wdn_r = np.ascontiguousarray(
        wdn.reshape(16, 128, 8, 8, 128).transpose(2, 0, 4, 3, 1))
    wo_r = np.ascontiguousarray(
        Wo.reshape(G, 4, 128, 8, 2, 128).transpose(0, 3, 2, 4, 1, 5))
    bo_r = np.ascontiguousarray(bo.reshape(16, 128).T)

    idents = np.eye(128, dtype=np.float32)
    identb = np.eye(128, dtype=ml_dtypes.bfloat16)
    ii = np.arange(128)[:, None]
    jj = np.arange(896)[None, :]
    maskbig = (ii <= jj - 384).astype(ml_dtypes.bfloat16)

    in_maps = []
    for core in range(8):
        b, pair = core // 2, core % 2
        rows = []
        for (pa, pb) in OWN_BLOCKS[pair]:
            rows.append(x[b, pa * 512:(pa + 1) * 512])
            rows.append(x[b, pb * 512:(pb + 1) * 512])
        x_own = np.ascontiguousarray(np.concatenate(rows, axis=0))
        in_maps.append({
            "x_ext": np.ascontiguousarray(x[b]),
            "x_own": x_own,
            "wq_r": wq_r, "wk": Wk, "wv": Wv, "wo_r": wo_r, "bo_r": bo_r,
            "wgu_r": wgu_r, "wdn_r": wdn_r,
            "idents": idents, "identb": identb, "maskbig": maskbig,
        })
    return in_maps


def kernel(**inputs):
    in_maps = _prep(inputs)
    nc, es = build_program()
    res = run_bass_kernel_spmd(nc, in_maps, core_ids=list(range(8)))
    out = np.zeros((B, S, D), dtype=np.float32)
    for core in range(8):
        b, pair = core // 2, core % 2
        o = res.results[core]["out"]
        r = 0
        for (pa, pb) in OWN_BLOCKS[pair]:
            out[b, pa * 512:(pa + 1) * 512] = o[r:r + 512]
            out[b, pb * 512:(pb + 1) * 512] = o[r + 512:r + 1024]
            r += 1024
    return out


# version 15
# speedup vs baseline: 1.0301x; 1.0089x over previous
import contextlib
import numpy as np
import ml_dtypes

import concourse.bass as bass
import concourse.mybir as mybir
from concourse.bass_utils import run_bass_kernel_spmd

F32 = mybir.dt.float32
BF16 = mybir.dt.bfloat16
AF = mybir.ActivationFunctionType
ALU = mybir.AluOpType

D = 2048
S = 4096
B = 4
NH = 16
G = 4
NKT = 32
DCH = 16
SCALE = float(1.0 / np.sqrt(128.0))
SHIFT = -12.0
OWN_BLOCKS = {0: [(0, 7), (1, 6)], 1: [(2, 5), (3, 4)]}


class Ctl:
    DMA_SEMS = ("sq", "gq", "xta", "xtb", "gu0", "gu1", "dn0", "dn1",
                "qo0", "qo1", "sta", "stb", "stc", "std",
                "pq0", "pq1", "pq2", "pq3", "pq4", "pq5", "pq6", "pq7",
                "st4", "st5", "st6", "st7")

    def __init__(self, nc, es):
        self.sems = {}
        self.counts = {}
        for name in ("pe", "act", "dve", "pool") + self.DMA_SEMS:
            self.sems[name] = es.enter_context(nc.semaphore("s_" + name))
            self.counts[name] = 0

    def inc(self, instr, name, n=1):
        self.counts[name] += n
        instr.then_inc(self.sems[name], n)
        return self.counts[name]

    def dinc(self, instr, name):
        return self.inc(instr, name, 16)

    def wait(self, eng, name, val):
        if val and val > 0:
            eng.wait_ge(self.sems[name], val)

    def snapshot(self):
        return dict(self.counts)

    def restore(self, snap):
        self.counts = dict(snap)


def build_program(stop_after=None, dumps=()):
    nc = bass.Bass("TRN2", target_bir_lowering=False)
    es = contextlib.ExitStack()

    x_ext = nc.dram_tensor("x_ext", [S, D], F32, kind="ExternalInput")
    x_own = nc.dram_tensor("x_own", [2048, D], F32, kind="ExternalInput")
    wq_r = nc.dram_tensor("wq_r", [NH, 128, DCH, 128], F32, kind="ExternalInput")
    wk_n = nc.dram_tensor("wk", [D, 512], F32, kind="ExternalInput")
    wv_n = nc.dram_tensor("wv", [D, 512], F32, kind="ExternalInput")
    wo_r = nc.dram_tensor("wo_r", [G, 8, 128, 2, 4, 128], F32, kind="ExternalInput")
    bo_r = nc.dram_tensor("bo_r", [128, 16], F32, kind="ExternalInput")
    wgu_r = nc.dram_tensor("wgu_r", [128, 128, DCH, 128], F32, kind="ExternalInput")
    wdn_r = nc.dram_tensor("wdn_r", [8, 16, 128, 8, 128], F32, kind="ExternalInput")
    idents = nc.dram_tensor("idents", [128, 128], F32, kind="ExternalInput")
    identb = nc.dram_tensor("identb", [128, 128], BF16, kind="ExternalInput")
    maskbig = nc.dram_tensor("maskbig", [128, 896], BF16, kind="ExternalInput")
    out = nc.dram_tensor("out", [2048, D], F32, kind="ExternalOutput")

    dbg = {}
    for name, cols, dt in dumps:
        dbg[name] = nc.dram_tensor("dbg_" + name, [128, cols], dt, kind="ExternalOutput")

    ctl = Ctl(nc, es)
    pe, act, dve, sy, gq = nc.tensor, nc.scalar, nc.vector, nc.sync, nc.gpsimd

    sbuf = lambda n, sh, dt: es.enter_context(nc.sbuf_tensor(n, sh, dt))
    psum = lambda n, sh, dt: es.enter_context(nc.psum_tensor(n, sh, dt))

    kT = [sbuf(f"kT{g}", [128, NKT * 128], BF16) for g in range(G)]
    vT = [sbuf(f"vT{g}", [128, NKT * 128], BF16) for g in range(G)]
    ident_f = sbuf("ident_f", [128, 128], F32)
    maskb = sbuf("maskb", [128, 896], BF16)
    bo_s = sbuf("bo_s", [128, 16], F32)
    ones_b = sbuf("ones_b", [128, 128], BF16)
    stat = sbuf("stat", [128, 32], F32)
    cpool = sbuf("cpool", [128, 4], F32)
    xnl = sbuf("xnl", [128, 1024], F32)
    xt = [sbuf(f"xt{i}", [128, 1024], F32) for i in range(2)]

    pA = psum("pA", [128, 512], F32)
    pB = psum("pB", [128, 512], F32)
    pC = psum("pC", [128, 512], F32)
    pD = psum("pD", [128, 512], F32)
    pS = [psum(f"pS{i}", [128, 512], F32) for i in range(2)]
    pDen = psum("pDen", [128, 512], F32)


    def dump(name, src_ap):
        if name in dbg:
            t = ctl.dinc(sy.dma_start(out=dbg[name].ap()[:, :], in_=src_ap), "sq")
            sy.wait_ge(ctl.sems["sq"], t)

    ctl.dinc(sy.dma_start(out=ident_f[:, :], in_=idents.ap()[:, :]), "sq")
    ctl.dinc(sy.dma_start(out=maskb[:, :], in_=maskbig.ap()[:, :]), "sq")
    ctl.dinc(sy.dma_start(out=bo_s[:, :], in_=bo_r.ap()[:, :]), "sq")
    dve.memset(ones_b[:, :], 1.0)
    dve.memset(cpool[:, 0:1], 1e-5)
    dve.memset(cpool[:, 1:2], SHIFT)
    sy.wait_ge(ctl.sems["sq"], ctl.counts["sq"])
    nc.all_engine_barrier()

    sw_engines = [pe, act, dve, gq]
    sw_idx = [e.compute_val(e.partition_id() % 2) for e in sw_engines]

    pt_evac = [0, 0]

    def emit_ln_tile(x_dram, r0, dest3d, prev, bufs=None):
        bxt, bstat, bxnl, lds = bufs if bufs is not None else (xt, stat, xnl, ("xta", "xtb"))
        ctl.wait(sy, "act", prev.get("xapply", 0))
        t0 = ctl.dinc(sy.dma_start(out=bxt[0][:, :],
                                   in_=x_dram.ap()[r0:r0 + 128, 0:1024]), lds[0])
        t1 = ctl.dinc(sy.dma_start(out=bxt[1][:, :],
                                   in_=x_dram.ap()[r0:r0 + 128, 1024:2048]), lds[1])
        ctl.wait(dve, "act", prev.get("xapply", 0))
        ctl.wait(dve, lds[0], t0)
        dve.bn_stats(bstat[:, 0:6], bxt[0][:, 0:512])
        dve.bn_stats(bstat[:, 6:12], bxt[0][:, 512:1024])
        ctl.wait(dve, lds[1], t1)
        dve.bn_stats(bstat[:, 12:18], bxt[1][:, 0:512])
        tb = ctl.inc(dve.bn_stats(bstat[:, 18:24], bxt[1][:, 512:1024]), "dve")
        ctl.wait(dve, "dve", tb)
        ta = ctl.inc(dve.bn_aggr(bstat[:, 24:26], bstat[:, 0:24]), "dve")
        ctl.wait(act, "dve", ta)
        ts = ctl.inc(act.activation(bstat[:, 26:27], bstat[:, 25:26], AF.Sqrt,
                                    bias=cpool[:, 0:1], scale=1.0), "act")
        ctl.wait(dve, "act", ts)
        tr = ctl.inc(dve.reciprocal(bstat[:, 27:28], bstat[:, 26:27]), "dve")
        ctl.wait(dve, "dve", tr)
        tn = ctl.inc(dve.scalar_tensor_tensor(bstat[:, 28:29], bstat[:, 24:25], -1.0,
                                              bstat[:, 27:28], ALU.mult, ALU.mult), "dve")
        ctl.wait(act, "dve", tn)
        tp = prev.get("transpose", 0)
        tx = 0
        for half in range(2):
            ctl.wait(act, "pe", tp)
            tx = ctl.inc(act.activation(bxnl[:, :], bxt[half][:, :], AF.Identity,
                                        bias=bstat[:, 28:29], scale=bstat[:, 27:28]), "act")
            for q4 in range(2):
                qg = half * 2 + q4
                pq = qg % 2
                pt_bank = pS[pq]
                ctl.wait(pe, "act", tx)
                ctl.wait(pe, "act", pt_evac[pq])
                for j in range(4):
                    dcl = q4 * 4 + j
                    tp = ctl.inc(pe.transpose(pt_bank[:, j * 128:(j + 1) * 128],
                                              bxnl[:, dcl * 128:(dcl + 1) * 128],
                                              ident_f[:, :]), "pe")
                ctl.wait(act, "pe", tp)
                pt_evac[pq] = ctl.inc(act.copy(
                    dest3d(qg),
                    pt_bank[:, :].rearrange("p (a b) -> p a b", a=4)), "act")
        return {"xapply": tx, "transpose": tp, "evac": max(pt_evac)}

    xn = sbuf("xn", [128, DCH * 1024], BF16)

    with contextlib.ExitStack() as p1es:
        wkb = p1es.enter_context(nc.sbuf_tensor("wkb", [128, DCH * 512], BF16))
        wvb = p1es.enter_context(nc.sbuf_tensor("wvb", [128, DCH * 512], BF16))
        xnT = p1es.enter_context(nc.sbuf_tensor("xnT", [128, DCH * 512], BF16))
        xt2 = [p1es.enter_context(nc.sbuf_tensor(f"xt2_{i}", [128, 1024], F32))
               for i in range(2)]
        stat2 = p1es.enter_context(nc.sbuf_tensor("stat2", [128, 32], F32))
        xnl2 = p1es.enter_context(nc.sbuf_tensor("xnl2", [128, 1024], F32))
        xt3 = [p1es.enter_context(nc.sbuf_tensor(f"xt3_{i}", [128, 1024], F32))
               for i in range(2)]
        stat3 = p1es.enter_context(nc.sbuf_tensor("stat3", [128, 32], F32))
        xnl3 = p1es.enter_context(nc.sbuf_tensor("xnl3", [128, 1024], F32))
        scr = p1es.enter_context(nc.sbuf_tensor("scr", [128, 192], BF16))
        scr_init = ctl.inc(gq.memset(scr[:, :], 0.0), "pool")
        bufsets = [(xt, stat, xnl, ("xta", "xtb")),
                   (xt2, stat2, xnl2, ("qo0", "qo1")),
                   (xt3, stat3, xnl3, ("stc", "std"))]
        xnT3 = xnT[:, :].rearrange("p (d c) -> p d c", d=DCH)
        tw = 0
        for d in range(DCH):
            ctl.dinc(gq.dma_start(out=wkb[:, d * 512:(d + 1) * 512],
                                  in_=wk_n.ap()[d * 128:(d + 1) * 128, :]), "gq")
            tw = ctl.dinc(gq.dma_start(out=wvb[:, d * 512:(d + 1) * 512],
                                       in_=wv_n.ap()[d * 128:(d + 1) * 128, :]), "gq")

        prevs = [{}, {}, {}]
        k_free = {0: 0, 1: 0}
        v_free = {0: 0, 1: 0}
        ctl.wait(pe, "gq", tw)
        for st in range(8):
            for tt in range(4):
                r0 = st * 512 + tt * 128
                bs = (st * 4 + tt) % 3

                def dest3d(q4, tt=tt):
                    base = xnT[:, :].rearrange("p (d c) -> p d c", d=DCH)
                    return base[:, q4 * 4:(q4 + 1) * 4, tt * 128:(tt + 1) * 128]

                prevs[bs] = emit_ln_tile(x_ext, r0, dest3d, prevs[bs],
                                         bufsets[bs])
            prev = {"evac": max(pv.get("evac", 0) for pv in prevs)}
            own_map = {0: (0, 0), 2: (1, 0), 5: (1, 1), 7: (0, 1)}
            if st in own_map:
                own_pair, half = own_map[st]
                snap = ctl.snapshot()
                ends = []
                for i in nc.Switch(engines=sw_engines, index=sw_idx, n=2):
                    ctl.restore(snap)
                    if i == own_pair:
                        ctl.wait(gq, "act", prev["evac"])
                        for d_ in range(DCH):
                            ctl.inc(gq.tensor_copy(
                                xn[:, d_ * 1024 + half * 512: d_ * 1024 + (half + 1) * 512],
                                xnT[:, d_ * 512:(d_ + 1) * 512]), "pool")
                    else:
                        ctl.wait(gq, "pool", scr_init)
                        base_slot = {0: 0, 2: 16, 5: 32, 7: 48}[st]
                        for d_ in range(DCH):
                            sl = (base_slot + d_) * 2
                            ctl.inc(gq.tensor_copy(scr[:, sl:sl + 2],
                                                   scr[:, 128:130]), "pool")
                    ends.append(ctl.snapshot())
                assert ends[0] == ends[1]
                ctl.restore(ends[0])
                ctl.wait(act, "pool", ctl.counts["pool"])
            ctl.wait(pe, "act", prev["evac"])
            for g in range(G):
                bank = g % 2
                pk = pA if bank == 0 else pB
                ctl.wait(pe, "act", k_free[bank])
                tm = 0
                for d in range(DCH):
                    tm = ctl.inc(pe.matmul(pk[:, :],
                                           wkb[:, d * 512 + g * 128: d * 512 + (g + 1) * 128],
                                           xnT[:, d * 512:(d + 1) * 512],
                                           start=(d == 0), stop=(d == DCH - 1)), "pe")
                ctl.wait(act, "pe", tm)
                k_free[bank] = ctl.inc(act.copy(kT[g][:, st * 512:(st + 1) * 512], pk[:, :]), "act")
            for tt in range(4):
                bank = tt % 2
                pv = pC if bank == 0 else pD
                ctl.wait(pe, "act", v_free[bank])
                tm = 0
                for d in range(DCH):
                    tm = ctl.inc(pe.matmul(pv[:, :],
                                           xnT[:, d * 512 + tt * 128: d * 512 + (tt + 1) * 128],
                                           wvb[:, d * 512:(d + 1) * 512],
                                           start=(d == 0), stop=(d == DCH - 1)), "pe")
                ctl.wait(act, "pe", tm)
                tv = 0
                kt_idx = st * 4 + tt
                for g in range(G):
                    tv = ctl.inc(act.copy(vT[g][:, kt_idx * 128:(kt_idx + 1) * 128],
                                          pv[:, g * 128:(g + 1) * 128]), "act")
                v_free[bank] = tv

        p1_prev0 = prevs[0]

    dump("kT0", kT[0][:, :])
    dump("vT0", vT[0][:, :])
    pe_after_p1 = ctl.counts["pe"]
    act_after_p1 = ctl.counts["act"]
    if stop_after == "p1":
        nc.all_engine_barrier()
        sy.wait_ge(ctl.sems["sq"], ctl.counts["sq"])
        nc.finalize()
        return nc, es

    acc = [sbuf(f"acc{d}", [128, 1024], F32) for d in range(DCH)]
    hg = [sbuf(f"hg{c}", [128, 1024], BF16) for c in range(8)]
    ctxg = [sbuf(f"ctx{h}", [128, 1024], BF16) for h in range(4)]
    qh0 = None
    ring = [sbuf(f"wr{i}", [128, 1024], BF16) for i in range(4)]
    RSEMS = ["gu0", "gu1", "dn0", "dn1"]
    ring_state = {"next": 0, "free": [pe_after_p1] * 4}

    def ring_load(src_ap):
        i = ring_state["next"]
        ring_state["next"] = (i + 1) % len(ring)
        ctl.wait(gq, "pe", ring_state["free"][i])
        tl = ctl.dinc(gq.dma_start(out=ring[i][:, :], in_=src_ap), RSEMS[i])
        return i, tl

    def ring_done(i):
        ring_state["free"][i] = ctl.counts["pe"]

    attb = [hg[0][:, 0:512], hg[0][:, 512:1024], hg[2][:, 0:512], hg[2][:, 512:1024]]
    qh = [hg[1][:, :], hg[3][:, :]]
    bcf = xt[1][:, 512:1024]
    rd = xt[1][0:1, 0:512]

    def xn_s(d, s):
        return xn[:, d * 1024 + s * 512: d * 1024 + (s + 1) * 512]

    for p in range(2):
        prev = p1_prev0 if p == 0 else {}
        if p > 0:
            for tile in range(8):
                r0 = p * 1024 + tile * 128

                def dest3d(q4, tile=tile):
                    base = xn[:, :].rearrange("p (d c) -> p d c", d=DCH)
                    return base[:, q4 * 4:(q4 + 1) * 4, tile * 128:(tile + 1) * 128]

                prev = emit_ln_tile(x_own, r0, dest3d, prev)
        ln_evac_last = prev.get("evac", 0)
        if p == 0:
            ctl.wait(pe, "pool", ctl.counts["pool"])
        if stop_after == "ln0":
            nc.all_engine_barrier()
            dump("xn", xn[:, 0:4096])
            sy.wait_ge(ctl.sems["sq"], ctl.counts["sq"])
            nc.finalize()
            return nc, es

        h_war = 0
        psd_free = [None, None, None, None]
        psd_banks = [pC, pD, pS[0], pS[1]]
        h_tok = {}
        ctl.wait(pe, "act", max(ln_evac_last, act_after_p1))
        for grp in range(8):
            prev_h = {}
            for c8 in range(8):
                c = grp * 8 + c8
                for which, blk in (("g", c), ("u", 64 + c)):
                    loads = [ring_load(wgu_r.ap()[blk, :, hf * 8:(hf + 1) * 8, :])
                             for hf in range(2)]
                    tmm_s = {0: 0, 1: 0}
                    for half in range(2):
                        slot, tl = loads[half]
                        ctl.wait(pe, RSEMS[slot], tl)
                        for s in range(2):
                            pg = pA if s == 0 else pB
                            if half == 0:
                                if which == "g":
                                    ctl.wait(pe, "dve", prev_h.get(s, 0))
                                else:
                                    ctl.wait(pe, "act", h_tok.get(("silu", c8, s), 0))
                            for j in range(8):
                                d = half * 8 + j
                                tmm_s[s] = ctl.inc(pe.matmul(
                                    pg[:, :],
                                    ring[slot][:, j * 128:(j + 1) * 128],
                                    xn_s(d, s),
                                    start=(d == 0), stop=(d == DCH - 1)), "pe")
                        ring_done(slot)
                    for s in range(2):
                        pg = pA if s == 0 else pB
                        if which == "g":
                            ctl.wait(act, "pe", tmm_s[s])
                            h_tok[("silu", c8, s)] = ctl.inc(act.activation(
                                hg[c8][:, s * 512:(s + 1) * 512], pg[:, :],
                                AF.Silu, bias=0.0, scale=1.0), "act")
                        else:
                            ctl.wait(dve, "pe", tmm_s[s])
                            ctl.wait(dve, "act", h_tok[("silu", c8, s)])
                            ctl.wait(dve, "pe", h_war)
                            h_tok[(c8, s)] = ctl.inc(dve.tensor_mul(
                                hg[c8][:, s * 512:(s + 1) * 512],
                                hg[c8][:, s * 512:(s + 1) * 512],
                                pg[:, :]), "dve")
                prev_h = {0: h_tok[(c8, 0)], 1: h_tok[(c8, 1)]}
            for o in range(16):
                rslot, tl = ring_load(wdn_r.ap()[grp, o, :, :, :])
                bank = o % 2
                ctl.wait(pe, RSEMS[rslot], tl)
                for s in range(2):
                    pd = psd_banks[bank * 2 + s]
                    pf = psd_free[bank * 2 + s]
                    if pf is not None:
                        ctl.wait(pe, pf[0], pf[1])
                    tmm = 0
                    for c8 in range(8):
                        ctl.wait(pe, "dve", h_tok[(c8, s)])
                        tmm = ctl.inc(pe.matmul(
                            pd[:, :],
                            ring[rslot][:, c8 * 128:(c8 + 1) * 128],
                            hg[c8][:, s * 512:(s + 1) * 512],
                            start=(c8 == 0), stop=(c8 == 7)), "pe")
                    if grp == 0:
                        ctl.wait(act, "pe", tmm)
                        te = ctl.inc(act.activation(
                            acc[o][:, s * 512:(s + 1) * 512], pd[:, :],
                            AF.Identity, bias=bo_s[:, o:o + 1], scale=1.0), "act")
                        psd_free[bank * 2 + s] = ("act", te)
                    else:
                        ctl.wait(dve, "pe", tmm)
                        te = ctl.inc(dve.tensor_add(
                            acc[o][:, s * 512:(s + 1) * 512],
                            acc[o][:, s * 512:(s + 1) * 512], pd[:, :]), "dve")
                        psd_free[bank * 2 + s] = ("dve", te)
                ring_done(rslot)
            h_war = ctl.counts["pe"]
        if stop_after == "ffn0":
            nc.all_engine_barrier()
            for i, nme in ((0, "acc0"), (8, "acc8")):
                dump(nme, acc[i][:, :])
            sy.wait_ge(ctl.sems["sq"], ctl.counts["sq"])
            nc.finalize()
            return nc, es

        p_exts = {0: [4 * (OWN_BLOCKS[0][p][0] + 1), 4 * (OWN_BLOCKS[0][p][1] + 1)],
                  1: [4 * (OWN_BLOCKS[1][p][0] + 1), 4 * (OWN_BLOCKS[1][p][1] + 1)]}

        dve_at_body_start = ctl.counts["dve"]
        act_at_body_start = ctl.counts["act"]
        ring_snap = dict(ring_state, free=list(ring_state["free"]))

        def body(pair):
            e_a, e_b = p_exts[pair]
            for g in range(G):
                ctl.wait(pe, "dve", dve_at_body_start if g == 0 else ctl.counts["dve"])
                ctl.wait(pe, "act", act_at_body_start if g == 0 else ctl.counts["act"])
                prev_q = [0, 0]
                norm = {"bank": [0, 0], "tcp": 0}
                qbc = [0]
                prev_exp_last = 0
                pending = []

                def flush_upto(qmax):
                    while pending and pending[0][0] <= qmax:
                        pending.pop(0)[1]()

                def flush_one():
                    if pending:
                        pending.pop(0)[1]()

                def flush_pending():
                    while pending:
                        pending.pop(0)[1]()

                def emit_qproj(h, qdst):
                    nonlocal prev_q
                    tmm_q = 0
                    loads = [ring_load(wq_r.ap()[h, :, hf * 8:(hf + 1) * 8, :])
                             for hf in range(2)]
                    for hf in range(2):
                        slot, tl = loads[hf]
                        ctl.wait(pe, RSEMS[slot], tl)
                        if hf == 0:
                            ctl.wait(pe, "act", prev_q[1])
                        for s in range(2):
                            pq = pA if s == 0 else pB
                            for j in range(8):
                                d = hf * 8 + j
                                tmm_q = ctl.inc(pe.matmul(
                                    pq[:, :],
                                    ring[slot][:, j * 128:(j + 1) * 128],
                                    xn_s(d, s),
                                    start=(d == 0), stop=(d == DCH - 1)), "pe")
                        ring_done(slot)
                    ctl.wait(act, "pe", tmm_q)
                    tq0 = ctl.inc(act.copy(qdst[:, 0:512], pA[:, :]), "act")
                    tq1 = ctl.inc(act.copy(qdst[:, 512:1024], pB[:, :]), "act")
                    prev_q = [tq0, tq1]
                    return tq0, tq1

                tq_next = emit_qproj(g * 4, qh[0])
                for hh in range(4):
                    h = g * 4 + hh
                    qdst = qh[hh % 2]
                    tq0, tq1 = tq_next
                    for qbi, ext in ((0, e_a), (1, e_b)):
                        qc = qbi * 512
                        exp_t = {}
                        mask_t = {}
                        ctx_t = {}
                        den_t = {}
                        pCD = [pC, pD][qbc[0] % 2]
                        qbc[0] += 1

                        def emit_ctxden(k, ext=ext, g=g, pCD=pCD):
                            if k == 0:
                                ctl.wait(pe, "dve", norm["bank"][pCD is pD])
                            if k in mask_t:
                                ctl.wait(pe, "dve", mask_t[k])
                            else:
                                ctl.wait(pe, "act", exp_t[k])
                            ctx_t[k] = ctl.inc(pe.matmul(
                                pCD[:, :],
                                vT[g][:, k * 128:(k + 1) * 128],
                                attb[k % 4],
                                start=(k == 0), stop=(k == ext - 1)), "pe")
                            if k == 0:
                                ctl.wait(pe, "act", norm["tcp"])
                            den_t[k] = ctl.inc(pe.matmul(
                                pDen[:, :], ones_b[:, :], attb[k % 4],
                                start=(k == 0), stop=(k == ext - 1)), "pe")

                        for kt in range(ext):
                            par = kt % 2
                            if kt >= 2:
                                ctl.wait(pe, "act", exp_t[kt - 2])
                            else:
                                ctl.wait(pe, "act", prev_exp_last)
                            if kt == 0:
                                ctl.wait(pe, "act", tq0 if qbi == 0 else tq1)
                            tsm = ctl.inc(pe.matmul(
                                pS[par][:, :],
                                kT[g][:, kt * 128:(kt + 1) * 128],
                                qdst[:, qc:qc + 512],
                                start=True, stop=True), "pe")
                            if kt == 2:
                                flush_upto(qbc[0] - 3)
                                flush_one()
                            elif kt in (3, 4):
                                flush_one()
                            if kt >= 3:
                                emit_ctxden(kt - 3)
                            ctl.wait(act, "pe", tsm)
                            if kt >= 4:
                                ctl.wait(act, "pe", den_t[kt - 4])
                            exp_t[kt] = ctl.inc(act.activation(
                                attb[kt % 4], pS[par][:, :], AF.Exp,
                                bias=cpool[:, 1:2], scale=SCALE), "act")
                            if kt >= ext - 4:
                                i = kt - (ext - 4)
                                ctl.wait(dve, "act", exp_t[kt])
                                mask_t[kt] = ctl.inc(dve.tensor_mul(
                                    attb[kt % 4], attb[kt % 4],
                                    maskb[:, 384 - 128 * i: 896 - 128 * i]), "dve")
                        emit_ctxden(ext - 3)
                        emit_ctxden(ext - 2)
                        emit_ctxden(ext - 1)
                        prev_exp_last = exp_t[ext - 1]
                        if qbi == 0 and hh < 3:
                            tq_next = emit_qproj(g * 4 + hh + 1, qh[(hh + 1) % 2])

                        cell = {}

                        def st_ln(tden=den_t[ext - 1], cell=cell):
                            ctl.wait(act, "pe", tden)
                            ctl.wait(act, "dve", max(norm["bank"]))
                            cell["ln"] = ctl.inc(act.activation(
                                bcf, pDen[:, :], AF.Ln, bias=0.0, scale=1.0), "act")
                            norm["tcp"] = cell["ln"]

                        def st_exp(cell=cell):
                            ctl.wait(act, "act", cell["ln"])
                            cell["rc"] = ctl.inc(act.activation(
                                bcf, bcf, AF.Exp, bias=0.0, scale=-1.0), "act")

                        def st_norm(tctx_=ctx_t[ext - 1], hh=hh, qc=qc, pCD=pCD, cell=cell):
                            ctl.wait(dve, "act", cell["rc"])
                            ctl.wait(dve, "pe", tctx_)
                            norm["bank"][pCD is pD] = ctl.inc(dve.tensor_mul(
                                ctxg[hh][:, qc:qc + 512], pCD[:, :], bcf), "dve")
                        pending.extend([(qbc[0] - 1, st_ln), (qbc[0] - 1, st_exp),
                                        (qbc[0] - 1, st_norm)])
                flush_pending()

                attn_done_dve = ctl.counts["dve"]
                attn_done_act = ctl.counts["act"]
                psw_free = [None, None, None, None]
                psw_banks = [pC, pD, pS[0], pS[1]]
                first_wo = True
                for o2 in range(8):
                    rslot, tl = ring_load(wo_r.ap()[g, o2, :, :, :, :])
                    ctl.wait(pe, RSEMS[rslot], tl)
                    for j in range(2):
                        o = o2 * 2 + j
                        for s in range(2):
                            pw = psw_banks[j * 2 + s]
                            pf = psw_free[j * 2 + s]
                            if pf is not None:
                                ctl.wait(pe, pf[0], pf[1])
                            if first_wo:
                                ctl.wait(pe, "dve", attn_done_dve)
                                ctl.wait(pe, "act", attn_done_act)
                                first_wo = False
                            tmm = 0
                            for co in range(4):
                                tmm = ctl.inc(pe.matmul(
                                    pw[:, :],
                                    ring[rslot][:, (j * 4 + co) * 128:(j * 4 + co + 1) * 128],
                                    ctxg[co][:, s * 512:(s + 1) * 512],
                                    start=(co == 0), stop=(co == 3)), "pe")
                            ctl.wait(dve, "pe", tmm)
                            te = ctl.inc(dve.tensor_add(
                                acc[o][:, s * 512:(s + 1) * 512],
                                acc[o][:, s * 512:(s + 1) * 512], pw[:, :]), "dve")
                            psw_free[j * 2 + s] = ("dve", te)
                    ring_done(rslot)

        snap = ctl.snapshot()
        ends = []
        ring_ends = []
        for i in nc.Switch(engines=sw_engines, index=sw_idx, n=2):
            ctl.restore(snap)
            ring_state["next"] = ring_snap["next"]
            ring_state["free"] = list(ring_snap["free"])
            body(i)
            ends.append(ctl.snapshot())
            ring_ends.append((ring_state["next"], list(ring_state["free"])))
        assert ends[0] == ends[1], f"switch unbalanced: {ends[0]} vs {ends[1]}"
        assert ring_ends[0] == ring_ends[1]
        ctl.restore(ends[0])

        if stop_after == "attn0":
            nc.all_engine_barrier()
            for i, nme in ((0, "acc0"), (8, "acc8")):
                dump(nme, acc[i][:, :])
            sy.wait_ge(ctl.sems["sq"], ctl.counts["sq"])
            nc.finalize()
            return nc, es

        ctl.wait(pe, "dve", ctl.counts["dve"])
        ctl.wait(pe, "act", ctl.counts["act"])
        ctl.wait(sy, "pe", ctl.counts["pe"])
        ctl.wait(sy, "dve", ctl.counts["dve"])
        ctl.wait(gq, "pe", ctl.counts["pe"])
        ctl.wait(gq, "dve", ctl.counts["dve"])
        ctl.wait(gq, "act", ctl.counts["act"])
        xslots = [hg[i].bitcast(F32)[:, 0:512] for i in range(4)] + [
            xt[0][:, 0:512], xt[0][:, 512:1024],
            xt[1][:, 0:512], xt[1][:, 512:1024]]
        LD = [f"pq{i}" for i in range(8)]
        ST = ["sta", "stb", "stc", "std", "st4", "st5", "st6", "st7"]
        st_free = [0] * 8
        tr_free = [0, 0]
        trb = [pA, pB]
        for qi in range(32):
            tile, q4 = qi // 4, qi % 4
            r0 = p * 1024 + tile * 128
            slot = qi % 8
            bank = qi % 2
            ctl.wait(gq, ST[slot], st_free[slot])
            tl = ctl.dinc(gq.dma_start(
                out=xslots[slot],
                in_=x_own.ap()[r0:r0 + 128, q4 * 512:(q4 + 1) * 512]), LD[slot])
            ctl.wait(pe, "dve", tr_free[bank])
            tp = 0
            for f in range(4):
                fg = q4 * 4 + f
                tp = ctl.inc(pe.transpose(trb[bank][:, f * 128:(f + 1) * 128],
                                          acc[fg][:, tile * 128:(tile + 1) * 128],
                                          ident_f[:, :]), "pe")
            ctl.wait(dve, "pe", tp)
            ctl.wait(dve, LD[slot], tl)
            ta = ctl.inc(dve.tensor_add(xslots[slot], xslots[slot],
                                        trb[bank][:, :]), "dve")
            tr_free[bank] = ta
            ctl.wait(sy, "dve", ta)
            st_free[slot] = ctl.dinc(sy.dma_start(
                out=out.ap()[r0:r0 + 128, q4 * 512:(q4 + 1) * 512],
                in_=xslots[slot]), ST[slot])
        for s_ in ST:
            sy.wait_ge(ctl.sems[s_], ctl.counts[s_])
        nc.all_engine_barrier()

    for s_ in ("sta", "stb", "stc", "std", "st4", "st5", "st6", "st7"):
        sy.wait_ge(ctl.sems[s_], ctl.counts[s_])
    sy.wait_ge(ctl.sems["sq"], ctl.counts["sq"])
    nc.finalize()
    return nc, es



def _prep(inputs):
    x = np.asarray(inputs["L0_attn_x"], dtype=np.float32)
    Wq = np.asarray(inputs["L0_attn_Wq"], dtype=np.float32)
    Wk = np.asarray(inputs["L0_attn_Wk"], dtype=np.float32)
    Wv = np.asarray(inputs["L0_attn_Wv"], dtype=np.float32)
    Wo = np.asarray(inputs["L0_attn_Wo"], dtype=np.float32)
    bo = np.asarray(inputs["L0_attn_bo"], dtype=np.float32)
    wgu = np.asarray(inputs["L0_ffn_w_gate_up"], dtype=np.float32)
    wdn = np.asarray(inputs["L0_ffn_w_down"], dtype=np.float32)

    wq_r = np.ascontiguousarray(Wq.reshape(DCH, 128, NH, 128).transpose(2, 1, 0, 3))
    wgu_r = np.ascontiguousarray(
        wgu.reshape(128, 128, DCH, 128).transpose(0, 3, 2, 1))
    wdn_r = np.ascontiguousarray(
        wdn.reshape(16, 128, 8, 8, 128).transpose(2, 0, 4, 3, 1))
    wo_r = np.ascontiguousarray(
        Wo.reshape(G, 4, 128, 8, 2, 128).transpose(0, 3, 2, 4, 1, 5))
    bo_r = np.ascontiguousarray(bo.reshape(16, 128).T)

    idents = np.eye(128, dtype=np.float32)
    identb = np.eye(128, dtype=ml_dtypes.bfloat16)
    ii = np.arange(128)[:, None]
    jj = np.arange(896)[None, :]
    maskbig = (ii <= jj - 384).astype(ml_dtypes.bfloat16)

    in_maps = []
    for core in range(8):
        b, pair = core // 2, core % 2
        rows = []
        for (pa, pb) in OWN_BLOCKS[pair]:
            rows.append(x[b, pa * 512:(pa + 1) * 512])
            rows.append(x[b, pb * 512:(pb + 1) * 512])
        x_own = np.ascontiguousarray(np.concatenate(rows, axis=0))
        in_maps.append({
            "x_ext": np.ascontiguousarray(x[b]),
            "x_own": x_own,
            "wq_r": wq_r, "wk": Wk, "wv": Wv, "wo_r": wo_r, "bo_r": bo_r,
            "wgu_r": wgu_r, "wdn_r": wdn_r,
            "idents": idents, "identb": identb, "maskbig": maskbig,
        })
    return in_maps


def kernel(**inputs):
    in_maps = _prep(inputs)
    nc, es = build_program()
    res = run_bass_kernel_spmd(nc, in_maps, core_ids=list(range(8)))
    out = np.zeros((B, S, D), dtype=np.float32)
    for core in range(8):
        b, pair = core // 2, core % 2
        o = res.results[core]["out"]
        r = 0
        for (pa, pb) in OWN_BLOCKS[pair]:
            out[b, pa * 512:(pa + 1) * 512] = o[r:r + 512]
            out[b, pb * 512:(pb + 1) * 512] = o[r + 512:r + 1024]
            r += 1024
    return out
